# version 11 (final)
# speedup vs baseline: 1.0306x; 1.0060x over previous
import contextlib
import math
import numpy as np
import concourse.bass as bass
import concourse.mybir as mybir
from concourse.bass_utils import run_bass_kernel_spmd

F32 = mybir.dt.float32
BF16 = mybir.dt.bfloat16
AF = mybir.ActivationFunctionType
ALU = mybir.AluOpType
AX = mybir.AxisListType

T = 2048
D = 1024
NT = T // 128
ND = D // 128


class _Op:
    __slots__ = ("eng", "fn", "deps", "dma_slot", "inc", "cnt", "small",
                 "scope")

    def __init__(self, eng, fn, dma_slot=None, small=False, scope=None):
        self.eng = eng
        self.fn = fn
        self.deps = []
        self.dma_slot = dma_slot
        self.inc = False
        self.cnt = None
        self.small = small
        self.scope = scope


class Core:
    ENGS = ("sync", "tensor", "vector", "scalar", "gpsimd")
    PARANOID = False
    SAFE_INC = True

    def __init__(self, nc, es):
        self.nc = nc
        self.es = es
        self.ops = []
        self.last_op = {}
        self.last_w = {}
        self.readers = {}
        self.esem = {e: es.enter_context(nc.semaphore("sem_" + e))
                     for e in self.ENGS}
        self.dsem = {}
        self._sb_names = set()
        self.ps = [es.enter_context(nc.psum_tensor(f"ps{i}", [128, 512], F32))
                   for i in range(7)]
        self.pbf = es.enter_context(nc.psum_tensor("pbf", [128, 1024], BF16))
        self.master = None
        self.master_addr = 0
        self.master_bytes = 0
        self.region_off = 0
        self.region_hwm = 0
        self.phase_gate = None
        self._init_consts()

    def set_master(self, nbytes):
        m = self.sb("master", [128, nbytes], mybir.dt.uint8)
        self.master = m
        self.master_addr = self.nc.lookup_mloc(m).addr
        self.master_bytes = nbytes

    def phase_barrier(self):
        deps = [v[1] for v in self.dsem.values() if v[1] is not None]
        deps += [op for op in self.last_op.values() if op is not None]
        g = self.emit("gpsimd", lambda e: e.nop(), deps=deps)
        self.phase_gate = g
        self.region_off = 0
        return g

    def emit(self, eng, fn, reads=(), writes=(), dma_slot=None, deps=(),
             small=False):
        op = _Op(eng, fn, dma_slot, small, getattr(self, "cur_scope", None))
        last_on_eng = self.last_op.get(eng)
        dd = set()
        for r in reads:
            w = self.last_w.get(r)
            if w is not None:
                dd.add(w)
        for r in writes:
            w = self.last_w.get(r)
            if w is not None:
                dd.add(w)
            for rd in self.readers.get(r, ()):
                dd.add(rd)
        for d in deps:
            if d is not None:
                dd.add(d)
        if self.phase_gate is not None and op is not self.phase_gate:
            dd.add(self.phase_gate)
        if dma_slot is not None:
            if dma_slot not in self.dsem:
                self.dsem[dma_slot] = [self.es.enter_context(
                    self.nc.semaphore("dma_" + dma_slot)), None]
            prev = self.dsem[dma_slot][1]
            if prev is not None:
                dd.add(prev)
            self.dsem[dma_slot][1] = op
        for d in list(dd):
            if d is op:
                dd.discard(d)
            elif d.dma_slot is None and d.eng == eng:
                if (not Core.PARANOID) and not (d is last_on_eng and d.small):
                    dd.discard(d)
                else:
                    d.inc = True
            else:
                d.inc = True
        op.deps = list(dd)
        for r in reads:
            self.readers.setdefault(r, []).append(op)
        for r in writes:
            self.last_w[r] = op
            self.readers[r] = []
        self.ops.append(op)
        self.last_op[eng] = op
        return op

    def finalize(self):
        nums = ([s.num for s in self.esem.values()] +
                [v[0].num for v in self.dsem.values()])
        rng = range(min(nums), max(nums) + 1)
        self.nc.gpsimd.dma_reset(rng)
        self.nc.gpsimd.sem_clear(rng)
        ecnt = {e: 0 for e in self.ENGS}
        dcnt = {s: 0 for s in self.dsem}
        for op in self.ops:
            if op.dma_slot is not None:
                dcnt[op.dma_slot] += 16
                op.cnt = dcnt[op.dma_slot]
                op.inc = True
            elif op.inc:
                ecnt[op.eng] += 1
                op.cnt = ecnt[op.eng]
        blk = self.es.enter_context(self.nc.Block())

        def semof(op):
            if op.dma_slot is not None:
                return self.dsem[op.dma_slot][0]
            return self.esem[op.eng]

        def replay(eng_name):
            my = [op for op in self.ops if op.eng == eng_name]

            def run(eng):
                hw = {}
                ride = False
                for oi, op in enumerate(my):
                    wants = {}
                    for d in op.deps:
                        assert d.cnt is not None
                        s = semof(d)
                        k = id(s)
                        if d.cnt > wants.get(k, (0, None))[0]:
                            wants[k] = (d.cnt, s)
                    for k, (v, s) in wants.items():
                        if Core.PARANOID or hw.get(k, 0) < v:
                            eng.wait_ge(s, v)
                            hw[k] = v
                    if op.scope is not None:
                        with self.nc.named_scope(op.scope):
                            ins = op.fn(eng)
                    else:
                        ins = op.fn(eng)
                    if ride:
                        ins.then_inc(self.esem[op.eng], 1)
                        ride = False
                        continue
                    if op.dma_slot is not None:
                        ins.then_inc(self.dsem[op.dma_slot][0], 16)
                    elif op.inc:
                        if Core.SAFE_INC and eng_name in ("vector", "scalar",
                                                          "gpsimd"):
                            nxt = my[oi + 1] if oi + 1 < len(my) else None
                            if (nxt is not None and not nxt.deps
                                    and nxt.dma_slot is None and not nxt.inc):
                                ride = True
                                continue
                            dmy = self.dummy[eng_name]
                            if eng_name == "scalar":
                                ins = eng.activation(dmy[:], dmy[:], AF.Identity)
                            else:
                                ins = eng.tensor_copy(dmy[:], dmy[:])
                        ins.then_inc(self.esem[op.eng], 1)
            return run

        for e in self.ENGS:
            getattr(blk, e)(replay(e))
        nops = len(self.ops)
        nincs = sum(1 for o in self.ops if o.inc)
        return nops, nincs

    def sb(self, name, shape, dtype):
        assert name not in self._sb_names, name
        self._sb_names.add(name)
        if self.master is not None:
            per = int(np.prod(shape[1:])) * mybir.dt.size(dtype)
            off = (self.region_off + 31) & ~31
            assert off + per <= self.master_bytes, (name, off, per)
            self.region_off = off + per
            self.region_hwm = max(self.region_hwm, self.region_off)
            return self.nc.alloc_sbuf_tensor_at(
                name, shape, dtype, offset=self.master_addr + off)
        return self.es.enter_context(self.nc.sbuf_tensor(name, shape, dtype))

    DEBUG = False

    def dump(self, name, ap, reads):
        if not Core.DEBUG:
            return
        dt = ap.dtype
        do = self.nc.dram_tensor("dbg_" + name, list(ap.shape), dt,
                                 kind="ExternalOutput").ap()
        self.emit("sync", lambda e: e.dma_start(out=do, in_=ap),
                  reads=list(reads), dma_slot="stdbg")

    def sb_overlay(self, name, shape, dtype, base, byte_off=0):
        addr = self.nc.lookup_mloc(base).addr + byte_off
        return self.nc.alloc_sbuf_tensor_at(name, shape, dtype, offset=addr)

    def _init_consts(self):
        self.dummy = {e: self.sb("dmy_" + e, [1, 32], F32)
                      for e in ("vector", "scalar", "gpsimd")}
        for t in self.dummy.values():
            self.nc.gpsimd.memset(t[:], 0.0)
        self.ident = self.sb("ident", [128, 128], F32)
        self.identb = self.sb("identb", [128, 128], BF16)
        self.ones_r = self.sb("ones_r", [1, 512], BF16)
        self.ones_rf = self.sb("ones_rf", [1, 128], F32)
        self.ones_c = self.sb("ones_c", [128, 1], BF16)

        def mk(g):
            g.memset(self.ident[:], 0.0)
            g.affine_select(out=self.ident[:], in_=self.ident[:],
                            compare_op=ALU.not_equal, fill=1.0, base=0,
                            pattern=[[-1, 128]], channel_multiplier=1)
            g.memset(self.identb[:], 0.0)
            g.affine_select(out=self.identb[:], in_=self.identb[:],
                            compare_op=ALU.not_equal, fill=1.0, base=0,
                            pattern=[[-1, 128]], channel_multiplier=1)
            g.memset(self.ones_r[:], 1.0)
            g.memset(self.ones_rf[:], 1.0)
            return g.memset(self.ones_c[:], 1.0)
        self.emit("gpsimd", mk, writes=["consts"])

    def load_vec_cols(self, name, dram_vec, n, bank=6, tmp=None, tmpres=None):
        ncol = n // 128
        if tmp is None:
            tmp = self.sb(name + "_t", [ncol, 128], F32)
        tr = tmpres or (name + "_t")
        cols = self.sb(name + "_c", [128, ncol], F32)
        pb = self.ps[bank]
        self.emit("sync", lambda e: e.dma_start(
            out=tmp[0:ncol, :], in_=dram_vec.rearrange("(c p) -> c p", p=128)),
            writes=[tr], dma_slot="vload")
        self.emit("tensor", lambda e: e.transpose(
            pb[0:128, 0:ncol], tmp[0:ncol, :], self.ident[0:ncol, 0:ncol]),
            reads=[tr, "consts"], writes=["psB%d" % bank])
        self.emit("vector", lambda e: e.tensor_copy(cols[:], pb[0:128, 0:ncol]),
                  reads=["psB%d" % bank], writes=[name + "_c"])
        return cols

    def load_vec_row(self, name, dram_vec, n, dtype=BF16):
        row = self.sb(name + "_r", [1, n], dtype)
        self.emit("gpsimd", lambda e: e.dma_start(
            out=row[:], in_=dram_vec.rearrange("(a n) -> a n", a=1)),
            writes=[name + "_r"], dma_slot="vload_g")
        return row

    def end_barrier(self):
        lasts = [v[1] for k, v in self.dsem.items() if k.startswith("st")]
        self.emit("sync", lambda e: e.nop(), deps=lasts)


def emit_l1ffn(c: Core, xin, xout, wd, pfx="l1f"):
    nc = c.nc

    nw_c = c.load_vec_cols(pfx + "nw", wd["norm_w"], D)
    nb_c = c.load_vec_cols(pfx + "nb", wd["norm_b"], D)
    qb_c = c.load_vec_cols(pfx + "qb", wd["q_b"], D)
    kb_c = c.load_vec_cols(pfx + "kb", wd["k_b"], D)
    vb_r = c.load_vec_row(pfx + "vb", wd["v_b"], D)
    ob_r = c.load_vec_row(pfx + "ob", wd["o_b"], D)

    xb = c.sb(pfx + "x", [128, 2 * D], F32)
    scol = c.sb(pfx + "S", [128, NT], F32)
    qcol = c.sb(pfx + "Q", [128, NT], F32)
    sqs = c.sb(pfx + "sq", [128, D], BF16)
    stat = c.sb(pfx + "stat", [128, 8 * NT], F32)
    zn = c.sb(pfx + "zn", [128, D], F32)
    hn = c.sb(pfx + "hn", [128, ND * T], BF16)
    wnat = c.sb(pfx + "wnat", [128, ND * D], BF16)
    wT = c.sb(pfx + "wT", [128, ND * D], BF16)
    owT = c.sb(pfx + "owT", [128, ND * D], BF16)
    qf = c.sb(pfx + "qf", [128, ND * T], BF16)
    kf = c.sb(pfx + "kf", [128, ND * T], BF16)
    vtok = c.sb(pfx + "v", [128, NT * D], BF16)
    expb = c.sb_overlay(pfx + "exp", [128, 3 * 512], BF16, wnat, 0)
    denr = c.sb_overlay(pfx + "den", [1, 512], F32, wnat, 3 * 1024)
    bca = c.sb_overlay(pfx + "bca", [128, 512], F32, wnat, 3 * 1024 + 2048)
    zts = c.sb_overlay(pfx + "zt", [128, D], F32, wnat, 3 * 1024 + 2048 + 2048)

    def hnc(cc):
        return hn[:, cc * T:(cc + 1) * T]

    for tt in range(NT):
        sl = xb[:, (tt % 2) * D:(tt % 2) * D + D]
        xres = pfx + "xs%d" % (tt % 2)
        c.emit("sync", lambda e, s=sl, tt=tt: e.dma_start(
            out=s, in_=xin[tt * 128:(tt + 1) * 128, :]),
            writes=[xres], dma_slot="x%d" % (tt % 2))
        c.emit("scalar", lambda e, s=sl, tt=tt: e.activation(
            sqs[:], s, AF.Square, accum_out=qcol[:, tt:tt + 1]),
            reads=[xres], writes=[pfx + "sqs", pfx + "Q%d" % tt])
        c.emit("vector", lambda e, s=sl, tt=tt: e.reduce_sum(
            scol[:, tt:tt + 1], s, axis=AX.X),
            reads=[xres], writes=[pfx + "S%d" % tt])

    m0 = stat[:, 0:NT]
    rstd0 = stat[:, NT:2 * NT]
    s01 = stat[:, 2 * NT:3 * NT]
    b01 = stat[:, 3 * NT:4 * NT]
    t1 = stat[:, 4 * NT:5 * NT]
    t2 = stat[:, 5 * NT:6 * NT]
    b00 = stat[:, 6 * NT:7 * NT]
    t3 = stat[:, 7 * NT:8 * NT]
    statdeps = ([pfx + "Q%d" % t for t in range(NT)] +
                [pfx + "S%d" % t for t in range(NT)])

    V = lambda fn, r, w: c.emit("vector", fn, reads=r, writes=w, small=True)
    A = lambda fn, r, w: c.emit("scalar", fn, reads=r, writes=w, small=True)
    p = pfx
    V(lambda e: e.tensor_scalar_mul(m0, scol[:, :], 1.0 / D), statdeps, [p+"m0"])
    V(lambda e: e.tensor_scalar_mul(t3, qcol[:, :], 1.0 / D), statdeps, [p+"t3"])
    V(lambda e: e.tensor_mul(t2, m0, m0), [p+"m0"], [p+"t2"])
    V(lambda e: e.tensor_sub(t1, t3, t2), [p+"t3", p+"t2"], [p+"t1"])
    V(lambda e: e.tensor_scalar_add(t2, t1, 1e-5), [p+"t1"], [p+"t2a"])
    A(lambda e: e.activation(t3, t2, AF.Ln), [p+"t2a"], [p+"t3"])
    A(lambda e: e.activation(rstd0, t3, AF.Exp, scale=-0.5), [p+"t3"], [p+"rstd0"])
    V(lambda e: e.tensor_mul(t2, rstd0, rstd0), [p+"rstd0"], [p+"t2b"])
    V(lambda e: e.tensor_mul(t2, t1, t2), [p+"t1", p+"t2b"], [p+"t2c"])
    V(lambda e: e.tensor_scalar_add(t2, t2, 1e-5), [p+"t2c"], [p+"t2d"])
    A(lambda e: e.activation(t3, t2, AF.Ln), [p+"t2d"], [p+"t3b"])
    A(lambda e: e.activation(t2, t3, AF.Exp, scale=-0.5), [p+"t3b"], [p+"t2e"])
    V(lambda e: e.tensor_mul(s01, rstd0, t2), [p+"rstd0", p+"t2e"], [p+"s01"])
    V(lambda e: e.tensor_mul(b01, m0, s01), [p+"m0", p+"s01"], [p+"b01"])
    V(lambda e: e.tensor_scalar_mul(b01, b01, -1.0), [p+"b01"], [p+"b01"])
    V(lambda e: e.tensor_mul(b00, m0, rstd0), [p+"m0", p+"rstd0"], [p+"b00"])
    c.emit("vector", lambda e: e.tensor_scalar_mul(b00, b00, -1.0),
           reads=[p+"b00", p+"s01", p+"b01", p+"rstd0", p+"m0"],
           writes=[p+"b00", pfx + "cols"], small=True)
    c.dump(pfx + "stat", stat[:], [pfx + "cols"])
    c.dump(pfx + "scol", scol[:], [pfx + "S%d" % t_ for t_ in range(NT)])
    c.dump(pfx + "qcol", qcol[:], [pfx + "Q%d" % t_ for t_ in range(NT)])

    for tt in range(NT):
        sl = xb[:, (tt % 2) * D:(tt % 2) * D + D]
        xres = pfx + "xs%d" % (tt % 2)
        c.emit("sync", lambda e, s=sl, tt=tt: e.dma_start(
            out=s, in_=xin[tt * 128:(tt + 1) * 128, :]),
            writes=[xres], dma_slot="x%d" % (tt % 2))
        znl = zn[:, 0:D]
        zres = pfx + "zn"
        c.emit("scalar", lambda e, s=sl, z=znl, tt=tt: e.activation(
            z, s, AF.Identity, scale=s01[:, tt:tt + 1], bias=b01[:, tt:tt + 1]),
            reads=[xres, pfx + "cols"], writes=[zres])
        for cc in range(ND):
            bank = cc % 2
            c.emit("tensor", lambda e, z=znl, cc=cc, b=bank: e.transpose(
                c.ps[b][0:128, 0:128], z[:, cc * 128:(cc + 1) * 128],
                c.ident[:]),
                reads=[zres, "consts"], writes=["psB%d" % bank])
            c.emit("scalar", lambda e, cc=cc, tt=tt, b=bank: e.activation(
                hnc(cc)[:, tt * 128:(tt + 1) * 128], c.ps[b][0:128, 0:128],
                AF.Identity, scale=nw_c[:, cc:cc + 1], bias=nb_c[:, cc:cc + 1]),
                reads=["psB%d" % bank, pfx + "nw_c", pfx + "nb_c"],
                writes=[pfx + "hn%d" % cc])
    hnres = [pfx + "hn%d" % ccc for ccc in range(ND)]

    c.dump(pfx + "hn", hn[:], [pfx + "hn%d" % ccc for ccc in range(ND)])

    def wload(dram, dst, res):
        c.emit("gpsimd", lambda e: e.dma_start(
            out=dst.rearrange("p (c m) -> p c m", m=D),
            in_=dram.rearrange("(c p) m -> p c m", p=128)),
            writes=[res], dma_slot="wload")

    def wtrans(src, srcres, dst, dstres):
        for dc in range(ND):
            for mc in range(ND):
                c.emit("tensor", lambda e, dc=dc, mc=mc: e.transpose(
                    c.pbf[0:128, mc * 128:(mc + 1) * 128],
                    src[:, dc * D + mc * 128: dc * D + mc * 128 + 128],
                    c.identb[:]),
                    reads=[srcres, "consts"], writes=["psBF"])
            c.emit("vector", lambda e, dc=dc: e.tensor_copy(
                dst.rearrange("p (m d) -> p m d", d=D)[:, :, dc * 128:dc * 128 + 128],
                c.pbf[0:128, :].rearrange("p (m n) -> p m n", n=128)),
                reads=["psBF"], writes=[dstres])

    def qk_prod(which, bias_c, dstbuf):
        for oc in range(ND):
            for tc in range(4):
                b = (oc * 4 + tc) % 2
                mch = ND >> tc
                for mc in range(mch):
                    c.emit("tensor", lambda e, oc=oc, tc=tc, mc=mc, b=b, m=mch:
                           e.matmul(c.ps[b][:, :],
                                    wT[:, mc * D + oc * 128: mc * D + oc * 128 + 128],
                                    hnc(mc)[:, tc * 512:(tc + 1) * 512],
                                    start=(mc == 0), stop=(mc == m - 1)),
                           reads=[pfx + "wT", hnres[mc]], writes=["psB%d" % b])
                c.emit("scalar", lambda e, oc=oc, tc=tc, b=b: e.activation(
                    dstbuf[:, oc * T + tc * 512: oc * T + (tc) * 512 + 512],
                    c.ps[b][:, :], AF.Identity, scale=1.0,
                    bias=bias_c[:, oc:oc + 1]),
                    reads=["psB%d" % b, pfx + which + "b_c"],
                    writes=[pfx + which + "f_h%d_qc%d" % (oc, tc)])

    wload(wd["q_w"], wnat, pfx + "wnat")
    wtrans(wnat, pfx + "wnat", wT, pfx + "wT")
    qk_prod("q", qb_c, qf)
    c.dump(pfx + "qf", qf[:], [pfx + "qf_h%d_qc%d" % (h_, t_)
                               for h_ in range(ND) for t_ in range(4)])
    wload(wd["k_w"], wnat, pfx + "wnat")
    wtrans(wnat, pfx + "wnat", wT, pfx + "wT")
    qk_prod("k", kb_c, kf)
    c.dump(pfx + "kf", kf[:], [pfx + "kf_h%d_qc%d" % (h_, t_)
                               for h_ in range(ND) for t_ in range(4)])

    wload(wd["v_w"], wnat, pfx + "wnat")
    wtrans(wnat, pfx + "wnat", wT, pfx + "wT")
    for tt in range(NT):
        e_ = tt // 4
        mch = ND >> e_
        for vc in range(2):
            b = 2 + (tt * 2 + vc) % 2
            for mc in range(mch):
                c.emit("tensor", lambda e, tt=tt, vc=vc, mc=mc, b=b:
                       e.matmul(c.ps[b][:, :],
                                hnc(mc)[:, tt * 128:(tt + 1) * 128],
                                wT[:, mc * D + vc * 512: mc * D + vc * 512 + 512],
                                start=(mc == 0), stop=False),
                       reads=[pfx + "wT", hnres[mc]], writes=["psB%d" % b])
            c.emit("tensor", lambda e, vc=vc, b=b: e.matmul(
                c.ps[b][:, :], c.ones_r[0:1, 0:128],
                vb_r[0:1, vc * 512:(vc + 1) * 512], start=False, stop=True),
                reads=[pfx + "vb_r", "consts"], writes=["psB%d" % b])
            c.emit("vector", lambda e, tt=tt, vc=vc, b=b: e.tensor_copy(
                vtok[:, tt * D + vc * 512: tt * D + vc * 512 + 512],
                c.ps[b][:, :]),
                reads=["psB%d" % b], writes=[pfx + "v%d" % tt])

    c.dump(pfx + "vtok", vtok[:], [pfx + "v%d" % t_ for t_ in range(NT)])

    wload(wd["o_w"], wnat, pfx + "wnat")
    wtrans(wnat, pfx + "wnat", owT, pfx + "owT")

    c.emit("gpsimd", lambda e: e.nop(),
           writes=[pfx + "wnat", "exp0", "exp1", "exp2", pfx + "den",
                   pfx + "bca", pfx + "zts"])

    SC = 1.0 / 32.0
    for h in range(ND):
        for qc in range(4):
            qres = pfx + "qf_h%d_qc%d" % (h, qc)
            for kt in range(NT):
                sb_ = kt % 2
                er = kt % 3
                c.emit("tensor", lambda e, h=h, qc=qc, kt=kt, b=sb_: e.matmul(
                    c.ps[b][:, :],
                    kf[:, h * T + kt * 128: h * T + kt * 128 + 128],
                    qf[:, h * T + qc * 512: h * T + qc * 512 + 512],
                    start=True, stop=True),
                    reads=[pfx + "kf_h%d_qc%d" % (h, kt // 4), qres],
                    writes=["psB%d" % sb_])
                c.emit("scalar", lambda e, kt=kt, b=sb_, er=er: e.activation(
                    expb[:, er * 512:(er + 1) * 512], c.ps[b][:, :],
                    AF.Exp, scale=SC),
                    reads=["psB%d" % sb_], writes=["exp%d" % er])
                c.emit("tensor", lambda e, h=h, kt=kt, er=er: e.matmul(
                    c.ps[2][:, :],
                    vtok[:, kt * D + h * 128: kt * D + h * 128 + 128],
                    expb[:, er * 512:(er + 1) * 512],
                    start=(kt == 0), stop=(kt == NT - 1)),
                    reads=["exp%d" % er, pfx + "v%d" % kt], writes=["psB2"])
                c.emit("tensor", lambda e, kt=kt, er=er: e.matmul(
                    c.ps[3][0:1, :], c.ones_c[:],
                    expb[:, er * 512:(er + 1) * 512],
                    start=(kt == 0), stop=(kt == NT - 1)),
                    reads=["exp%d" % er, "consts"], writes=["psB3"])
            c.emit("scalar", lambda e: e.activation(
                denr[:], c.ps[3][0:1, :], AF.Ln),
                reads=["psB3"], writes=[pfx + "den"])
            c.emit("scalar", lambda e: e.activation(
                denr[:], denr[:], AF.Exp, scale=-1.0),
                reads=[pfx + "den"], writes=[pfx + "den"])
            c.emit("tensor", lambda e: e.matmul(
                c.ps[4][:, :], c.ones_rf[:], denr[:], start=True, stop=True),
                reads=[pfx + "den", "consts"], writes=["psB4"])
            c.emit("vector", lambda e: e.tensor_copy(bca[:], c.ps[4][:, :]),
                   reads=["psB4"], writes=[pfx + "bca"])
            c.emit("vector", lambda e, h=h, qc=qc: e.tensor_mul(
                qf[:, h * T + qc * 512: h * T + qc * 512 + 512],
                c.ps[2][:, :], bca[:]),
                reads=["psB2", pfx + "bca"], writes=[qres])

    c.dump(pfx + "ytn", qf[:], [pfx + "qf_h%d_qc%d" % (h_, t_)
                                for h_ in range(ND) for t_ in range(4)])

    for tt in range(NT):
        e_ = tt // 4
        m = D >> e_
        sl = xb[:, (tt % 2) * D:(tt % 2) * D + D]
        xres = pfx + "xs%d" % (tt % 2)
        c.emit("sync", lambda e, s=sl, tt=tt: e.dma_start(
            out=s, in_=xin[tt * 128:(tt + 1) * 128, :]),
            writes=[xres], dma_slot="x%d" % (tt % 2))
        ztl = zts[:, 0:D]
        c.emit("scalar", lambda e, s=sl, tt=tt: e.activation(
            ztl, s, AF.Identity, scale=rstd0[:, tt:tt + 1],
            bias=b00[:, tt:tt + 1]),
            reads=[xres, pfx + "cols"], writes=[pfx + "zts"])
        c.emit("vector", lambda e, s=sl: e.scalar_tensor_tensor(
            s, ztl, 0.5, s, op0=ALU.mult, op1=ALU.add),
            reads=[pfx + "zts", xres], writes=[xres])
        for oc in range(max(1, m // 512)):
            w_ = min(512, m)
            b = 5
            mch = m // 128
            for mc in range(mch):
                c.emit("tensor", lambda e, tt=tt, oc=oc, mc=mc, w_=w_, b=b:
                       e.matmul(c.ps[b][:, 0:w_],
                                qf[:, mc * T + tt * 128: mc * T + tt * 128 + 128],
                                owT[:, mc * D + oc * 512: mc * D + oc * 512 + w_],
                                start=(mc == 0), stop=False),
                       reads=[pfx + "qf_h%d_qc%d" % (mc, tt // 4), pfx + "owT"],
                       writes=["psB%d" % b])
            c.emit("tensor", lambda e, oc=oc, w_=w_, b=b: e.matmul(
                c.ps[b][:, 0:w_], c.ones_r[0:1, 0:128],
                ob_r[0:1, oc * 512: oc * 512 + w_], start=False, stop=True),
                reads=[pfx + "ob_r", "consts"], writes=["psB%d" % b])
            c.emit("vector", lambda e, s=sl, oc=oc, w_=w_, b=b:
                   e.scalar_tensor_tensor(
                       s[:, oc * 512: oc * 512 + w_], c.ps[b][:, 0:w_], 0.5,
                       s[:, oc * 512: oc * 512 + w_],
                       op0=ALU.mult, op1=ALU.add),
                   reads=["psB%d" % b, xres], writes=[xres])
        c.emit("sync", lambda e, s=sl, tt=tt: e.dma_start(
            out=xout[tt * 128:(tt + 1) * 128, :], in_=s),
            reads=[xres], dma_slot="st%d" % (tt % 2))


def _build_l1ffn_program():
    es = contextlib.ExitStack()
    with es:
        nc = bass.Bass("TRN2", target_bir_lowering=False, detect_race_conditions=False)
        xin = nc.dram_tensor("x", [T, D], F32, kind="ExternalInput").ap()
        xout = nc.dram_tensor("xo", [T, D], F32, kind="ExternalOutput").ap()
        wd = {}
        for k, shp in [("norm_w", [D]), ("norm_b", [D]),
                       ("q_w", [D, D]), ("q_b", [D]), ("k_w", [D, D]),
                       ("k_b", [D]), ("v_w", [D, D]), ("v_b", [D]),
                       ("o_w", [D, D]), ("o_b", [D])]:
            wd[k] = nc.dram_tensor(k, shp, F32, kind="ExternalInput").ap()
        c = Core(nc, es)
        emit_l1ffn(c, xin, xout, wd)
        c.end_barrier()
        stats = c.finalize()
    return nc, stats


def run_l1ffn(x_full, w):
    nc, stats = _build_l1ffn_program()
    in_maps = []
    for b in range(8):
        m = {"x": np.ascontiguousarray(x_full[b])}
        for k in ("norm_w", "norm_b", "q_w", "q_b", "k_w", "k_b",
                  "v_w", "v_b", "o_w", "o_b"):
            m[k] = w[k]
        in_maps.append(m)
    r = run_bass_kernel_spmd(nc, in_maps, core_ids=list(range(8)))
    return np.stack([r.results[b]["xo"] for b in range(8)])


def _build_l1attn_program():
    es = contextlib.ExitStack()
    with es:
        nc = bass.Bass("TRN2", target_bir_lowering=False, detect_race_conditions=False)
        xin = nc.dram_tensor("x", [T, D], F32, kind="ExternalInput").ap()
        xout = nc.dram_tensor("xo", [T, D], F32, kind="ExternalOutput").ap()
        wd = {}
        for k, shp in [("Wq", [D, D]), ("Wk", [256, D]), ("Wv", [256, D]),
                       ("Wo", [D, D]), ("q_gamma", [256]), ("k_gamma", [256])]:
            wd[k] = nc.dram_tensor(k, shp, F32, kind="ExternalInput").ap()
        c = Core(nc, es)
        emit_l1attn(c, xin, xout, wd)
        c.end_barrier()
        stats = c.finalize()
    return nc, stats


def run_l1attn(x_full, w):
    nc, stats = _build_l1attn_program()
    in_maps = []
    for b in range(8):
        m = {"x": np.ascontiguousarray(x_full[b])}
        m.update(w)
        in_maps.append(m)
    r = run_bass_kernel_spmd(nc, in_maps, core_ids=list(range(8)))
    return np.stack([r.results[b]["xo"] for b in range(8)])


_W0F = [("wqkv", [2, D, 1536]), ("q_scale", [2, 64]), ("k_scale", [2, 64]),
        ("wo", [2, 512, D]), ("ln_g", [2, D]), ("ln_b", [2, D]),
        ("w1", [2, D, 5460]), ("b1", [2, 5460]), ("w2", [2, 2730, D]),
        ("b2", [2, D])]
_W0A = [("Wqkv", [D, 3 * D]), ("Wout", [D, D]), ("bout", [D])]
_W1F = [("norm_w", [D]), ("norm_b", [D]), ("q_w", [D, D]), ("q_b", [D]),
        ("k_w", [D, D]), ("k_b", [D]), ("v_w", [D, D]), ("v_b", [D]),
        ("o_w", [D, D]), ("o_b", [D])]
_W1A = [("Wq", [D, D]), ("Wk", [256, D]), ("Wv", [256, D]), ("Wo", [D, D]),
        ("q_gamma", [256]), ("k_gamma", [256])]


def _build_fused_program():
    es = contextlib.ExitStack()
    with es:
        nc = bass.Bass("TRN2", target_bir_lowering=False,
                       detect_race_conditions=False)
        xin = nc.dram_tensor("x", [T, D], F32, kind="ExternalInput").ap()
        xout = nc.dram_tensor("xo", [T, D], F32, kind="ExternalOutput").ap()
        t1 = nc.dram_tensor("xt1", [T, D], F32, kind="Internal").ap()
        t2 = nc.dram_tensor("xt2", [T, D], F32, kind="Internal").ap()
        wds = {}
        for grp, lst in (("w0f", _W0F), ("w0a", _W0A),
                         ("w1f", _W1F), ("w1a", _W1A)):
            wds[grp] = {k: nc.dram_tensor(grp + "_" + k, shp, F32,
                                          kind="ExternalInput").ap()
                        for k, shp in lst}
        c = Core(nc, es)
        c.set_master((nc.sbuf_bytes_remaining - 256) & ~31)
        seq = [(emit_l0ffn, xin, t1, wds["w0f"], "f0"),
               (emit_l0attn, t1, t2, wds["w0a"], "a0"),
               (emit_l0ffn, t2, t1, wds["w0f"], "g0"),
               (emit_l1ffn, t1, t2, wds["w1f"], "f1"),
               (emit_l1attn, t2, t1, wds["w1a"], "a1"),
               (emit_l1ffn, t1, xout, wds["w1f"], "g1")]
        hwms = []
        for fn, xi, xo, wd, tag in seq:
            c.cur_scope = tag
            fn(c, xi, xo, wd, pfx=tag)
            hwms.append(c.region_hwm)
            c.region_hwm = 0
            c.phase_barrier()
        c.end_barrier()
        stats = c.finalize()
    return nc, (stats, hwms)


def kernel(**inputs):
    x = np.ascontiguousarray(np.asarray(inputs["L0_attn_x"], np.float32))
    assert inputs["L1_ffn_router_prob"].shape[2] == 4
    w = {}
    for grp, src, lst in (("w0f", "L0_ffn_", _W0F), ("w0a", "L0_attn_", _W0A),
                          ("w1f", "L1_ffn_", _W1F), ("w1a", "L1_attn_", _W1A)):
        for k, _ in lst:
            w[grp + "_" + k] = np.ascontiguousarray(
                np.asarray(inputs[src + k], np.float32))
    nc, _ = _build_fused_program()
    maps = []
    for b in range(8):
        m = {"x": np.ascontiguousarray(x[b])}
        m.update(w)
        maps.append(m)
    r = run_bass_kernel_spmd(nc, maps, core_ids=list(range(8)))
    return np.stack([r.results[b]["xo"] for b in range(8)])


def _kernel_unfused(**inputs):
    x = np.ascontiguousarray(np.asarray(inputs["L0_attn_x"], np.float32))
    assert inputs["L1_ffn_router_prob"].shape[2] == 4
    w0f = {k: np.ascontiguousarray(np.asarray(inputs["L0_ffn_" + k], np.float32))
           for k in ("wqkv", "q_scale", "k_scale", "wo", "ln_g", "ln_b",
                     "w1", "b1", "w2", "b2")}
    w0a = {"Wqkv": inputs["L0_attn_Wqkv"], "Wout": inputs["L0_attn_Wout"],
           "bout": inputs["L0_attn_bout"]}
    w0a = {k: np.ascontiguousarray(np.asarray(v, np.float32))
           for k, v in w0a.items()}
    w1f = {k.split("L1_ffn_")[1]: np.ascontiguousarray(
        np.asarray(inputs["L1_ffn_" + kk], np.float32))
        for k, kk in [("L1_ffn_" + n, n) for n in
                      ("norm_w", "norm_b", "q_w", "q_b", "k_w", "k_b",
                       "v_w", "v_b", "o_w", "o_b")]}
    w1a = {k: np.ascontiguousarray(np.asarray(inputs["L1_attn_" + k],
                                              np.float32))
           for k in ("Wq", "Wk", "Wv", "Wo", "q_gamma", "k_gamma")}

    nc_l0f, _ = _build_l0ffn_program()
    nc_l0a, _ = _build_l0attn_program()
    nc_l1f, _ = _build_l1ffn_program()
    nc_l1a, _ = _build_l1attn_program()

    def launch(nc, w):
        maps = []
        for b in range(8):
            m = {"x": np.ascontiguousarray(x[b])}
            m.update(w)
            maps.append(m)
        r = run_bass_kernel_spmd(nc, maps, core_ids=list(range(8)))
        return np.stack([r.results[b]["xo"] for b in range(8)])

    x = launch(nc_l0f, w0f)
    x = launch(nc_l0a, w0a)
    x = launch(nc_l0f, w0f)
    x = launch(nc_l1f, w1f)
    x = launch(nc_l1a, w1a)
    x = launch(nc_l1f, w1f)
    return x


if __name__ == "__main__":
    nc, (nops, nincs) = _build_l1ffn_program()
    print("l1ffn program ops:", nops, "incs:", nincs)


def cw_split(period):
    c1 = float(np.float32(int(period * 2 ** 11) / 2.0 ** 11))
    r = period - c1
    c2 = float(np.float32(int(r * 2 ** 22) / 2.0 ** 22))
    c3 = float(period - c1 - c2)
    return c1, c2, c3


PI = math.pi
CW1, CW2, CW3 = cw_split(2 * math.pi)


def emit_const_freq(c, pfx, fb, vals):
    for j, v in enumerate(vals):
        c.emit("gpsimd", lambda e, j=j, v=float(np.float32(v)):
               e.memset(fb[:, j:j + 1], v),
               writes=[pfx + "fb"], small=True)


def freq_bits_recip(n, base, denom):
    j = np.arange(n, dtype=np.float64)
    p32 = np.power(float(base), j / float(denom)).astype(np.float32)
    return (1.0 / p32.astype(np.float64)).astype(np.float32)


def emit_rope_tables(c, pfx, fb, nf, costabs, sintabs, res):
    tcol = c.sb(pfx + "tcol", [128, 1], F32)
    ang = c.sb(pfx + "ang", [128, nf], F32)
    ki = c.sb(pfx + "ki", [128, nf], mybir.dt.int32)
    kf = c.sb(pfx + "kf", [128, nf], F32)
    w1 = c.sb(pfx + "w1", [128, nf], F32)
    w2 = c.sb(pfx + "w2", [128, nf], F32)
    TC, AN, K_, KI = pfx + "tcol", pfx + "ang", pfx + "kf", pfx + "ki"
    W1, W2 = pfx + "w1", pfx + "w2"
    sm = nf < 128
    for tt in range(NT):
        c.emit("gpsimd", lambda e, tt=tt: e.iota(
            tcol[:], pattern=[[0, 1]], base=tt * 128, channel_multiplier=1,
            allow_small_or_imprecise_dtypes=True), writes=[TC], small=True)
        c.emit("vector", lambda e: e.tensor_scalar(
            ang[:], fb[:], tcol[0:128, 0:1], None, op0=ALU.mult),
            reads=[TC, res + "freq"], writes=[AN], small=sm)
        c.emit("vector", lambda e: e.tensor_copy(ki[:], ang[:].bitcast(
            mybir.dt.int32)), reads=[AN], writes=[KI], small=sm)
        c.emit("vector", lambda e: e.tensor_copy(ang[:], ki[:].bitcast(F32)),
               reads=[KI], writes=[AN], small=sm)
        for which, dst, phase in (("c", costabs, 0.25), ("s", sintabs, 0.0)):
            dsl = dst[:, tt * nf:(tt + 1) * nf]
            c.emit("vector", lambda e, ph=phase: e.tensor_scalar(
                kf[:], ang[:], 1.0 / (2 * PI), ph, op0=ALU.mult, op1=ALU.add),
                reads=[AN], writes=[K_], small=sm)
            c.emit("vector", lambda e: e.tensor_copy(ki[:], kf[:]),
                   reads=[K_], writes=[KI], small=sm)
            c.emit("vector", lambda e: e.tensor_copy(kf[:], ki[:]),
                   reads=[KI], writes=[K_], small=sm)
            c.emit("vector", lambda e: e.scalar_tensor_tensor(
                w1[:], kf[:], -CW1, ang[:], op0=ALU.mult, op1=ALU.add),
                reads=[K_, AN], writes=[W1], small=sm)
            c.emit("vector", lambda e: e.scalar_tensor_tensor(
                w2[:], kf[:], -CW2, w1[:], op0=ALU.mult, op1=ALU.add),
                reads=[W1, K_], writes=[W2], small=sm)
            c.emit("vector", lambda e: e.scalar_tensor_tensor(
                w1[:], kf[:], -CW3, w2[:], op0=ALU.mult, op1=ALU.add),
                reads=[W2, K_], writes=[W1], small=sm)
            if phase:
                c.emit("vector", lambda e: e.tensor_scalar_add(
                    w1[:], w1[:], PI / 2), reads=[W1], writes=[W1])
            c.emit("vector", lambda e: e.tensor_single_scalar(
                w2[:], w1[:], PI, ALU.is_gt), reads=[W1], writes=[W2], small=sm)
            c.emit("vector", lambda e: e.scalar_tensor_tensor(
                w1[:], w2[:], -2 * PI, w1[:], op0=ALU.mult, op1=ALU.add),
                reads=[W2, W1], writes=[W1], small=sm)
            c.emit("vector", lambda e: e.tensor_single_scalar(
                w2[:], w1[:], -PI, ALU.is_lt), reads=[W1], writes=[W2], small=sm)
            c.emit("vector", lambda e: e.scalar_tensor_tensor(
                w1[:], w2[:], 2 * PI, w1[:], op0=ALU.mult, op1=ALU.add),
                reads=[W2, W1], writes=[W1], small=sm)
            c.emit("vector", lambda e: e.tensor_scalar_min(w1[:], w1[:], PI),
                   reads=[W1], writes=[W1], small=sm)
            c.emit("vector", lambda e: e.tensor_scalar_max(w1[:], w1[:], -PI),
                   reads=[W1], writes=[W1], small=sm)
            c.emit("scalar", lambda e, d=dsl: e.activation(d, w1[:], AF.Sin),
                   reads=[W1], writes=[res + which + "%d" % tt])


def emit_rope_apply(c, pfx, qtile, nheads, dh, costab, sintab, tabres, out,
                    inres, outres, scr):
    nf = dh // 2
    a, b = scr
    sm = nf * nheads < 128
    AR, BR = pfx + "a", pfx + "b"

    def st1(h):
        qe = qtile[:, h * dh:(h + 1) * dh:2]
        return lambda e, qe=qe, h=h: e.tensor_mul(
            a[:, h * nf:(h + 1) * nf], qe, costab)

    def st2(h):
        qo = qtile[:, h * dh + 1:(h + 1) * dh:2]
        return lambda e, qo=qo, h=h: e.tensor_mul(
            b[:, h * nf:(h + 1) * nf], qo, sintab)

    def st3(h):
        re = out[:, h * dh:(h + 1) * dh:2]
        return lambda e, re=re, h=h: e.tensor_sub(
            re, a[:, h * nf:(h + 1) * nf], b[:, h * nf:(h + 1) * nf])

    def st4(h):
        qe = qtile[:, h * dh:(h + 1) * dh:2]
        return lambda e, qe=qe, h=h: e.tensor_mul(
            a[:, h * nf:(h + 1) * nf], qe, sintab)

    def st5(h):
        qo = qtile[:, h * dh + 1:(h + 1) * dh:2]
        return lambda e, qo=qo, h=h: e.tensor_mul(
            b[:, h * nf:(h + 1) * nf], qo, costab)

    def st6(h):
        ro = out[:, h * dh + 1:(h + 1) * dh:2]
        return lambda e, ro=ro, h=h: e.tensor_add(
            ro, a[:, h * nf:(h + 1) * nf], b[:, h * nf:(h + 1) * nf])

    for fns, rds, wrs in ((st1, [inres] + tabres, [AR]),
                          (st2, [inres] + tabres, [BR]),
                          (st3, [AR, BR], [outres]),
                          (st4, [inres] + tabres, [AR]),
                          (st5, [inres] + tabres, [BR]),
                          (st6, [AR, BR], [outres])):
        for h in range(nheads):
            c.emit("vector", fns(h), reads=rds, writes=wrs,
                   small=(sm and nheads == 1))


def emit_l1attn(c: Core, xin, xout, wd, pfx="l1a"):
    nc = c.nc
    H, dh, G = 4, 256, 1
    nf = dh // 2
    SC = 1.0 / 16.0

    fb = c.sb(pfx + "fb", [128, nf], F32)
    emit_const_freq(c, pfx + "gf", fb, freq_bits_recip(nf, 1e6, nf))
    c.emit("vector", lambda e: e.engine_nop(), reads=[pfx + "gffb"],
           writes=[pfx + "freq"], small=True)
    costabs = c.sb(pfx + "cost", [128, NT * nf], F32)
    sintabs = c.sb(pfx + "sint", [128, NT * nf], F32)
    emit_rope_tables(c, pfx + "tb", fb, nf, costabs, sintabs, pfx)

    gq_r = c.load_vec_row(pfx + "gq", wd["q_gamma"], dh, F32)
    gk_r = c.load_vec_row(pfx + "gk", wd["k_gamma"], dh, F32)
    gqb = c.sb(pfx + "gqb", [128, dh], F32)
    gkb = c.sb(pfx + "gkb", [128, dh], F32)
    for nm, row, dst in (("gq", gq_r, gqb), ("gk", gk_r, gkb)):
        c.emit("tensor", lambda e, row=row: e.matmul(
            c.ps[5][:, 0:dh], c.ones_rf[:], row[0:1, :], start=True, stop=True),
            reads=[pfx + nm + "_r", "consts"], writes=["psB5"])
        c.emit("vector", lambda e, dst=dst: e.tensor_copy(dst[:], c.ps[5][:, 0:dh]),
               reads=["psB5"], writes=[pfx + nm + "b"])

    xb = c.sb(pfx + "x", [128, 2 * D], F32)
    scol = c.sb(pfx + "S", [128, NT], F32)
    qcol = c.sb(pfx + "Q", [128, NT], F32)
    sqs = c.sb(pfx + "sq", [128, D], BF16)
    stat = c.sb(pfx + "stat", [128, 4 * NT], F32)
    zn = c.sb(pfx + "zn", [128, D], F32)
    hn = c.sb(pfx + "hn", [128, ND * T], BF16)
    wnat = c.sb(pfx + "wnat", [128, ND * D], BF16)
    wT = c.sb(pfx + "wT", [128, ND * D], BF16)
    woT = c.sb(pfx + "woT", [128, ND * D], BF16)
    qT = c.sb(pfx + "qT", [128, ND * T], BF16)
    kT = c.sb(pfx + "kT", [128, 2 * T], BF16)
    vtok = c.sb(pfx + "v", [128, NT * dh], BF16)
    qtile = c.sb(pfx + "qt", [128, D], F32)
    qrot = c.sb(pfx + "qr", [128, D], F32)
    ktile = c.sb(pfx + "kt", [128, dh], F32)
    krot = c.sb(pfx + "kr", [128, dh], F32)
    mscol = c.sb(pfx + "ms", [128, 8], F32)
    ra = c.sb(pfx + "ra", [128, 4 * nf], F32)
    rb = c.sb(pfx + "rb", [128, 4 * nf], F32)
    expb = c.sb(pfx + "exp", [128, 3 * 512], BF16)
    denr = c.sb(pfx + "den", [1, 512], F32)
    bca = c.sb(pfx + "bca", [128, 512], F32)

    def hnc(cc):
        return hn[:, cc * T:(cc + 1) * T]

    for tt in range(NT):
        sl = xb[:, (tt % 2) * D:(tt % 2) * D + D]
        xres = pfx + "xs%d" % (tt % 2)
        c.emit("sync", lambda e, s=sl, tt=tt: e.dma_start(
            out=s, in_=xin[tt * 128:(tt + 1) * 128, :]),
            writes=[xres], dma_slot="x%d" % (tt % 2))
        c.emit("scalar", lambda e, s=sl, tt=tt: e.activation(
            sqs[:], s, AF.Square, accum_out=qcol[:, tt:tt + 1]),
            reads=[xres], writes=[pfx + "sqs", pfx + "Q%d" % tt])
        c.emit("vector", lambda e, s=sl, tt=tt: e.reduce_sum(
            scol[:, tt:tt + 1], s, axis=AX.X),
            reads=[xres], writes=[pfx + "S%d" % tt])
    m0 = stat[:, 0:NT]
    rstd0 = stat[:, NT:2 * NT]
    b00 = stat[:, 2 * NT:3 * NT]
    t2 = stat[:, 3 * NT - NT:3 * NT]
    t2 = stat[:, 2 * NT:3 * NT]
    b00 = stat[:, 3 * NT:4 * NT]
    statdeps = ([pfx + "Q%d" % t for t in range(NT)] +
                [pfx + "S%d" % t for t in range(NT)])
    V = lambda fn, r, w: c.emit("vector", fn, reads=r, writes=w, small=True)
    A = lambda fn, r, w: c.emit("scalar", fn, reads=r, writes=w, small=True)
    p = pfx
    V(lambda e: e.tensor_scalar_mul(m0, scol[:, :], 1.0 / D), statdeps, [p+"m0"])
    V(lambda e: e.tensor_scalar_mul(t2, qcol[:, :], 1.0 / D), statdeps, [p+"t2"])
    V(lambda e: e.tensor_mul(rstd0, m0, m0), [p+"m0"], [p+"msq"])
    V(lambda e: e.tensor_sub(t2, t2, rstd0), [p+"t2", p+"msq"], [p+"v0"])
    V(lambda e: e.tensor_scalar_add(t2, t2, 1e-5), [p+"v0"], [p+"v0e"])
    A(lambda e: e.activation(rstd0, t2, AF.Ln), [p+"v0e"], [p+"ln"])
    A(lambda e: e.activation(rstd0, rstd0, AF.Exp, scale=-0.5), [p+"ln"], [p+"rstd0"])
    V(lambda e: e.tensor_mul(b00, m0, rstd0), [p+"m0", p+"rstd0"], [p+"b00a"])
    c.emit("vector", lambda e: e.tensor_scalar_mul(b00, b00, -1.0),
           reads=[p+"b00a", p+"rstd0"], writes=[pfx + "cols"], small=True)

    for tt in range(NT):
        sl = xb[:, (tt % 2) * D:(tt % 2) * D + D]
        xres = pfx + "xs%d" % (tt % 2)
        c.emit("sync", lambda e, s=sl, tt=tt: e.dma_start(
            out=s, in_=xin[tt * 128:(tt + 1) * 128, :]),
            writes=[xres], dma_slot="x%d" % (tt % 2))
        c.emit("scalar", lambda e, s=sl, tt=tt: e.activation(
            zn[:], s, AF.Identity, scale=rstd0[:, tt:tt + 1],
            bias=b00[:, tt:tt + 1]),
            reads=[xres, pfx + "cols"], writes=[pfx + "zn"])
        for cc in range(ND):
            bank = cc % 2
            c.emit("tensor", lambda e, cc=cc, b=bank: e.transpose(
                c.ps[b][0:128, 0:128], zn[:, cc * 128:(cc + 1) * 128],
                c.ident[:]),
                reads=[pfx + "zn", "consts"], writes=["psB%d" % bank])
            c.emit("vector", lambda e, cc=cc, tt=tt, b=bank: e.tensor_copy(
                hnc(cc)[:, tt * 128:(tt + 1) * 128], c.ps[b][0:128, 0:128]),
                reads=["psB%d" % bank], writes=[pfx + "hn%d" % cc])
    hnres = [pfx + "hn%d" % ccc for ccc in range(ND)]

    def wload(dram, dst, res, nchunk=ND):
        c.emit("gpsimd", lambda e: e.dma_start(
            out=dst.rearrange("p (c m) -> p c m", m=D)[:, 0:nchunk, :],
            in_=dram.rearrange("(c p) m -> p c m", p=128)),
            writes=[res], dma_slot="wload")

    def wtrans(src, srcres, dst, dstres, nchunk=ND, mwidth=D):
        for dc in range(nchunk):
            for mc in range(mwidth // 128):
                c.emit("tensor", lambda e, dc=dc, mc=mc: e.transpose(
                    c.pbf[0:128, mc * 128:(mc + 1) * 128],
                    src[:, dc * D + mc * 128: dc * D + mc * 128 + 128],
                    c.identb[:]),
                    reads=[srcres, "consts"], writes=["psBF"])
            c.emit("vector", lambda e, dc=dc, nm=mwidth // 128: e.tensor_copy(
                dst.rearrange("p (m d) -> p m d", d=nchunk * 128)[
                    :, 0:nm, dc * 128:dc * 128 + 128],
                c.pbf[0:128, 0:nm * 128].rearrange("p (m n) -> p m n", n=128)),
                reads=["psBF"], writes=[dstres])

    wload(wd["Wq"], wnat, pfx + "wnat")
    wtrans(wnat, pfx + "wnat", wT, pfx + "wT")
    wkT = c.sb(pfx + "wkT", [128, ND * dh], BF16)
    wvT = c.sb(pfx + "wvT", [128, ND * dh], BF16)
    for nm, dram, dstT in (("wk", wd["Wk"], wkT), ("wv", wd["Wv"], wvT)):
        c.emit("gpsimd", lambda e, dram=dram: e.dma_start(
            out=wnat.rearrange("p (c m) -> p c m", m=D)[:, 0:2, :],
            in_=dram.rearrange("(c p) m -> p c m", p=128)),
            writes=[pfx + "wnat"], dma_slot="wload")
        for dc in range(2):
            for mc in range(ND):
                c.emit("tensor", lambda e, dc=dc, mc=mc: e.transpose(
                    c.pbf[0:128, mc * 128:(mc + 1) * 128],
                    wnat[:, dc * D + mc * 128: dc * D + mc * 128 + 128],
                    c.identb[:]),
                    reads=[pfx + "wnat", "consts"], writes=["psBF"])
            c.emit("vector", lambda e, dc=dc, dstT=dstT: e.tensor_copy(
                dstT.rearrange("p (m d) -> p m d", d=dh)[:, :, dc * 128:dc * 128 + 128],
                c.pbf[0:128, :].rearrange("p (m n) -> p m n", n=128)),
                reads=["psBF"], writes=[pfx + nm + "T"])

    for tt in range(NT):
        for oc in range(2):
            b = 2 + oc
            for mc in range(ND):
                c.emit("tensor", lambda e, tt=tt, oc=oc, mc=mc, b=b: e.matmul(
                    c.ps[b][:, :],
                    hnc(mc)[:, tt * 128:(tt + 1) * 128],
                    wT[:, mc * D + oc * 512: mc * D + oc * 512 + 512],
                    start=(mc == 0), stop=(mc == ND - 1)),
                    reads=[pfx + "wT", hnres[mc]], writes=["psB%d" % b])
            c.emit("vector", lambda e, oc=oc, b=b: e.tensor_copy(
                qtile[:, oc * 512:(oc + 1) * 512], c.ps[b][:, :]),
                reads=["psB%d" % b], writes=[pfx + "qtile"])
        for mc in range(ND):
            c.emit("tensor", lambda e, tt=tt, mc=mc: e.matmul(
                c.ps[4][:, 0:dh],
                hnc(mc)[:, tt * 128:(tt + 1) * 128],
                wkT[:, mc * dh: (mc + 1) * dh],
                start=(mc == 0), stop=(mc == ND - 1)),
                reads=[pfx + "wkT", hnres[mc]], writes=["psB4"])
        c.emit("vector", lambda e: e.tensor_copy(ktile[:], c.ps[4][:, 0:dh]),
               reads=["psB4"], writes=[pfx + "ktile"])
        for mc in range(ND):
            c.emit("tensor", lambda e, tt=tt, mc=mc: e.matmul(
                c.ps[5][:, 0:dh],
                hnc(mc)[:, tt * 128:(tt + 1) * 128],
                wvT[:, mc * dh: (mc + 1) * dh],
                start=(mc == 0), stop=(mc == ND - 1)),
                reads=[pfx + "wvT", hnres[mc]], writes=["psB5"])
        c.emit("vector", lambda e, tt=tt: e.tensor_copy(
            vtok[:, tt * dh:(tt + 1) * dh], c.ps[5][:, 0:dh]),
            reads=["psB5"], writes=[pfx + "v%d" % tt])

        for h in range(H):
            c.emit("scalar", lambda e, h=h: e.activation(
                sqs[:, 0:dh], qtile[:, h * dh:(h + 1) * dh], AF.Square,
                accum_out=mscol[:, h:h + 1]),
                reads=[pfx + "qtile"], writes=[pfx + "sqs", pfx + "msq%d" % h])
        c.emit("scalar", lambda e: e.activation(
            sqs[:, 0:dh], ktile[:], AF.Square, accum_out=mscol[:, 4:5]),
            reads=[pfx + "ktile"], writes=[pfx + "sqs", pfx + "msk"])
        V(lambda e: e.tensor_scalar(
            mscol[:, 5:6], mscol[:, 0:1], 1.0, 0.0, op0=ALU.mult, op1=ALU.add),
          [pfx + "msq0"], [pfx + "dummy"])
        V(lambda e: e.tensor_scalar(
            mscol[:, 0:5], mscol[:, 0:5], 1.0 / dh, 1e-6,
            op0=ALU.mult, op1=ALU.add),
          [pfx + "msq%d" % h_ for h_ in range(4)] + [pfx + "msk"], [pfx + "mse"])
        A(lambda e: e.activation(mscol[:, 0:5], mscol[:, 0:5], AF.Ln),
          [pfx + "mse"], [pfx + "msl"])
        A(lambda e: e.activation(mscol[:, 0:5], mscol[:, 0:5], AF.Exp, scale=-0.5),
          [pfx + "msl"], [pfx + "msr"])
        for h in range(H):
            c.emit("scalar", lambda e, h=h: e.activation(
                qtile[:, h * dh:(h + 1) * dh], qtile[:, h * dh:(h + 1) * dh],
                AF.Identity, scale=mscol[:, h:h + 1]),
                reads=[pfx + "msr", pfx + "qtile"], writes=[pfx + "qtile"])
            c.emit("vector", lambda e, h=h: e.tensor_mul(
                qtile[:, h * dh:(h + 1) * dh], qtile[:, h * dh:(h + 1) * dh],
                gqb[:]),
                reads=[pfx + "qtile", pfx + "gqb"], writes=[pfx + "qtile"])
        c.emit("scalar", lambda e: e.activation(
            ktile[:], ktile[:], AF.Identity, scale=mscol[:, 4:5]),
            reads=[pfx + "msr", pfx + "ktile"], writes=[pfx + "ktile"])
        c.emit("vector", lambda e: e.tensor_mul(ktile[:], ktile[:], gkb[:]),
               reads=[pfx + "ktile", pfx + "gkb"], writes=[pfx + "ktile"])
        ct = costabs[:, tt * nf:(tt + 1) * nf]
        st_ = sintabs[:, tt * nf:(tt + 1) * nf]
        tabres = [pfx + "c%d" % tt, pfx + "s%d" % tt]
        emit_rope_apply(c, pfx + "rq%d" % tt, qtile, H, dh, ct, st_, tabres,
                        qrot, pfx + "qtile", pfx + "qrot", (ra, rb))
        emit_rope_apply(c, pfx + "rk%d" % tt, ktile, 1, dh, ct, st_, tabres,
                        krot, pfx + "ktile", pfx + "krot", (ra, rb))
        for cc in range(ND):
            b = cc % 2
            c.emit("tensor", lambda e, cc=cc, b=b: e.transpose(
                c.ps[b][0:128, 0:128], qrot[:, cc * 128:(cc + 1) * 128],
                c.ident[:]),
                reads=[pfx + "qrot", "consts"], writes=["psB%d" % b])
            c.emit("vector", lambda e, cc=cc, tt=tt, b=b: e.tensor_copy(
                qT[:, cc * T + tt * 128: cc * T + tt * 128 + 128],
                c.ps[b][0:128, 0:128]),
                reads=["psB%d" % b], writes=[pfx + "qT_%d_%d" % (cc, tt // 4)])
        for cc in range(2):
            b = cc % 2
            c.emit("tensor", lambda e, cc=cc, b=b: e.transpose(
                c.ps[b][0:128, 0:128], krot[:, cc * 128:(cc + 1) * 128],
                c.ident[:]),
                reads=[pfx + "krot", "consts"], writes=["psB%d" % b])
            c.emit("vector", lambda e, cc=cc, tt=tt, b=b: e.tensor_copy(
                kT[:, cc * T + tt * 128: cc * T + tt * 128 + 128],
                c.ps[b][0:128, 0:128]),
                reads=["psB%d" % b], writes=[pfx + "kT%d" % tt])

    c.dump(pfx + "cost", costabs[:], [pfx + "c%d" % t_ for t_ in range(NT)])
    c.dump(pfx + "sint", sintabs[:], [pfx + "s%d" % t_ for t_ in range(NT)])
    c.dump(pfx + "hn", hn[:], hnres)
    c.dump(pfx + "qT", qT[:], [pfx + "qT_%d_%d" % (c_, t_)
                               for c_ in range(ND) for t_ in range(4)])
    c.dump(pfx + "kT", kT[:], [pfx + "kT%d" % t_ for t_ in range(NT)])
    c.dump(pfx + "vtok", vtok[:], [pfx + "v%d" % t_ for t_ in range(NT)])

    wload(wd["Wo"], wnat, pfx + "wnat")
    wtrans(wnat, pfx + "wnat", woT, pfx + "woT")

    W = 512
    for h in range(H):
        for qc in range(4):
            qres = [pfx + "qT_%d_%d" % (h * 2, qc), pfx + "qT_%d_%d" % (h * 2 + 1, qc)]
            kt0 = max(0, qc * 4 - 4)
            kts = list(range(kt0, qc * 4 + 4))
            for i, kt in enumerate(kts):
                sb_ = kt % 2
                er = kt % 3
                for cc in range(2):
                    c.emit("tensor", lambda e, h=h, qc=qc, kt=kt, cc=cc, b=sb_:
                           e.matmul(c.ps[b][:, :],
                                    kT[:, cc * T + kt * 128: cc * T + kt * 128 + 128],
                                    qT[:, (h * 2 + cc) * T + qc * 512:
                                       (h * 2 + cc) * T + qc * 512 + 512],
                                    start=(cc == 0), stop=(cc == 1)),
                           reads=[pfx + "kT%d" % kt, qres[cc]],
                           writes=["psB%d" % sb_])
                c.emit("scalar", lambda e, b=sb_, er=er: e.activation(
                    expb[:, er * 512:(er + 1) * 512], c.ps[b][:, :],
                    AF.Exp, scale=SC),
                    reads=["psB%d" % sb_], writes=["exp%d" % er])
                qb0 = qc * 512
                kb0 = kt * 128
                need_causal = kb0 + 127 > qb0 + 511 - 0 or kb0 + 127 >= qb0
                if kb0 + 127 >= qb0:
                    c.emit("gpsimd", lambda e, er=er, off=qb0 - kb0: e.affine_select(
                        out=expb[:, er * 512:(er + 1) * 512],
                        in_=expb[:, er * 512:(er + 1) * 512],
                        compare_op=ALU.is_ge, fill=0.0, base=off,
                        pattern=[[1, 512]], channel_multiplier=-1),
                        reads=["exp%d" % er], writes=["exp%d" % er])
                if kb0 < qb0:
                    c.emit("gpsimd", lambda e, er=er, off=512 - qb0 + kb0: e.affine_select(
                        out=expb[:, er * 512:(er + 1) * 512],
                        in_=expb[:, er * 512:(er + 1) * 512],
                        compare_op=ALU.is_ge, fill=0.0, base=off,
                        pattern=[[-1, 512]], channel_multiplier=1),
                        reads=["exp%d" % er], writes=["exp%d" % er])
                if Core.DEBUG and h == 0 and qc == 0 and kt == 0:
                    snapE = c.sb(pfx + "snapE", [128, 512], BF16)
                    c.emit("vector", lambda e, er=er: e.tensor_copy(
                        snapE[:], expb[:, er * 512:(er + 1) * 512]),
                        reads=["exp%d" % er], writes=[pfx + "snapE"])
                    c.dump(pfx + "expm", snapE[:], [pfx + "snapE"])
                for cc in range(2):
                    c.emit("tensor", lambda e, kt=kt, cc=cc, er=er, i=i,
                           n=len(kts): e.matmul(
                        c.ps[2 + cc][:, :],
                        vtok[:, kt * dh + cc * 128: kt * dh + cc * 128 + 128],
                        expb[:, er * 512:(er + 1) * 512],
                        start=(i == 0), stop=(i == n - 1)),
                        reads=["exp%d" % er, pfx + "v%d" % kt],
                        writes=["psB%d" % (2 + cc)])
                c.emit("tensor", lambda e, kt=kt, er=er, i=i, n=len(kts): e.matmul(
                    c.ps[4][0:1, :], c.ones_c[:],
                    expb[:, er * 512:(er + 1) * 512],
                    start=(i == 0), stop=(i == n - 1)),
                    reads=["exp%d" % er, "consts"], writes=["psB4"])
            if Core.DEBUG and h == 0 and qc == 0:
                snapD = c.sb(pfx + "snapD", [1, 512], F32)
                snapY = c.sb(pfx + "snapY", [128, 512], F32)
                c.emit("vector", lambda e: e.tensor_copy(snapD[:], c.ps[4][0:1, :]),
                       reads=["psB4"], writes=[pfx + "snapD"])
                c.dump(pfx + "den00", snapD[:], [pfx + "snapD"])
                c.emit("vector", lambda e: e.tensor_copy(snapY[:], c.ps[2][:, :]),
                       reads=["psB2"], writes=[pfx + "snapY"])
                c.dump(pfx + "y00", snapY[:], [pfx + "snapY"])
            c.emit("scalar", lambda e: e.activation(
                denr[:], c.ps[4][0:1, :], AF.Ln),
                reads=["psB4"], writes=[pfx + "den"])
            c.emit("scalar", lambda e: e.activation(
                denr[:], denr[:], AF.Exp, scale=-1.0),
                reads=[pfx + "den"], writes=[pfx + "den"])
            c.emit("tensor", lambda e: e.matmul(
                c.ps[5][:, :], c.ones_rf[:], denr[:], start=True, stop=True),
                reads=[pfx + "den", "consts"], writes=["psB5"])
            c.emit("vector", lambda e: e.tensor_copy(bca[:], c.ps[5][:, :]),
                   reads=["psB5"], writes=[pfx + "bca"])
            for cc in range(2):
                c.emit("vector", lambda e, h=h, qc=qc, cc=cc: e.tensor_mul(
                    qT[:, (h * 2 + cc) * T + qc * 512:
                       (h * 2 + cc) * T + qc * 512 + 512],
                    c.ps[2 + cc][:, :], bca[:]),
                    reads=["psB%d" % (2 + cc), pfx + "bca"], writes=[qres[cc]])

    c.dump(pfx + "ytn", qT[:], [pfx + "qT_%d_%d" % (c_, t_)
                                for c_ in range(ND) for t_ in range(4)])

    for tt in range(NT):
        sl = xb[:, (tt % 2) * D:(tt % 2) * D + D]
        xres = pfx + "xs%d" % (tt % 2)
        c.emit("sync", lambda e, s=sl, tt=tt: e.dma_start(
            out=s, in_=xin[tt * 128:(tt + 1) * 128, :]),
            writes=[xres], dma_slot="x%d" % (tt % 2))
        for oc in range(2):
            b = oc
            for mc in range(ND):
                c.emit("tensor", lambda e, tt=tt, oc=oc, mc=mc, b=b: e.matmul(
                    c.ps[b][:, :],
                    qT[:, mc * T + tt * 128: mc * T + tt * 128 + 128],
                    woT[:, mc * D + oc * 512: mc * D + oc * 512 + 512],
                    start=(mc == 0), stop=(mc == ND - 1)),
                    reads=[pfx + "qT_%d_%d" % (mc, tt // 4), pfx + "woT"],
                    writes=["psB%d" % b])
            c.emit("vector", lambda e, s=sl, oc=oc, b=b: e.tensor_add(
                s[:, oc * 512:(oc + 1) * 512], s[:, oc * 512:(oc + 1) * 512],
                c.ps[b][:, :]),
                reads=["psB%d" % b, xres], writes=[xres])
        c.emit("sync", lambda e, s=sl, tt=tt: e.dma_start(
            out=xout[tt * 128:(tt + 1) * 128, :], in_=s),
            reads=[xres], dma_slot="st%d" % (tt % 2))


def emit_l0attn(c: Core, xin, xout, wd, pfx="l0a"):
    nc = c.nc
    H, dh = 16, 64
    nf = dh // 2
    SC = 1.0 / 8.0

    bout_r = c.load_vec_row(pfx + "bout", wd["bout"], D)

    fb = c.sb(pfx + "fb", [128, nf], F32)
    th = np.power(1e4, (1.0 - np.arange(nf, dtype=np.float64)) / 32.0
                  ).astype(np.float32)
    emit_const_freq(c, pfx + "gf", fb, th)
    c.emit("vector", lambda e: e.engine_nop(), reads=[pfx + "gffb"],
           writes=[pfx + "freq"], small=True)
    costabs = c.sb(pfx + "cost", [128, NT * nf], F32)
    sintabs = c.sb(pfx + "sint", [128, NT * nf], F32)
    emit_rope_tables(c, pfx + "tb", fb, nf, costabs, sintabs, pfx)

    xb = c.sb(pfx + "x", [128, 2 * D], F32)
    scol = c.sb(pfx + "S", [128, NT], F32)
    qcol = c.sb(pfx + "Q", [128, NT], F32)
    sqs = c.sb(pfx + "sq", [128, D], BF16)
    stat = c.sb(pfx + "stat", [128, 4 * NT], F32)
    zn = c.sb(pfx + "zn", [128, D], F32)
    hn = c.sb(pfx + "hn", [128, ND * T], BF16)
    wnat = c.sb(pfx + "wnat", [128, ND * D], BF16)
    qTu = c.sb(pfx + "qTu", [128, ND * T], BF16)
    kTu = c.sb(pfx + "kTu", [128, ND * T], BF16)
    vtok = c.sb(pfx + "v", [128, NT * D], BF16)
    qtile = c.sb(pfx + "qt", [128, D], F32)
    qrot = c.sb(pfx + "qr", [128, D], F32)
    ra = c.sb(pfx + "ra", [128, H * nf], F32)
    rb = c.sb(pfx + "rb", [128, H * nf], F32)
    den = c.sb(pfx + "den", [128, H * NT], F32)
    denp = c.sb(pfx + "denp", [128, 4], F32)
    junk = c.sb(pfx + "junk", [128, 512], BF16)
    denrow = c.sb(pfx + "dr", [1, 512], F32)
    denrec = c.sb(pfx + "drr", [1, 512], F32)
    bca = c.sb(pfx + "bca", [128, 512], F32)
    expb = c.sb(pfx + "exp", [128, 3 * 512], BF16)
    qTr = c.sb_overlay(pfx + "qTr", [128, ND * T], BF16, qTu)
    kTr = c.sb_overlay(pfx + "kTr", [128, ND * T], BF16, kTu)
    yTn = c.sb_overlay(pfx + "yTn", [128, ND * T], BF16, hn)

    def hnc(cc):
        return hn[:, cc * T:(cc + 1) * T]

    for tt in range(NT):
        sl = xb[:, (tt % 2) * D:(tt % 2) * D + D]
        xres = pfx + "xs%d" % (tt % 2)
        c.emit("sync", lambda e, s=sl, tt=tt: e.dma_start(
            out=s, in_=xin[tt * 128:(tt + 1) * 128, :]),
            writes=[xres], dma_slot="x%d" % (tt % 2))
        c.emit("scalar", lambda e, s=sl, tt=tt: e.activation(
            sqs[:], s, AF.Square, accum_out=qcol[:, tt:tt + 1]),
            reads=[xres], writes=[pfx + "sqs", pfx + "Q%d" % tt])
        c.emit("vector", lambda e, s=sl, tt=tt: e.reduce_sum(
            scol[:, tt:tt + 1], s, axis=AX.X),
            reads=[xres], writes=[pfx + "S%d" % tt])
    m0 = stat[:, 0:NT]
    rstd0 = stat[:, NT:2 * NT]
    t2 = stat[:, 2 * NT:3 * NT]
    b00 = stat[:, 3 * NT:4 * NT]
    statdeps = ([pfx + "Q%d" % t for t in range(NT)] +
                [pfx + "S%d" % t for t in range(NT)])
    V = lambda fn, r, w: c.emit("vector", fn, reads=r, writes=w, small=True)
    A = lambda fn, r, w: c.emit("scalar", fn, reads=r, writes=w, small=True)
    p = pfx
    V(lambda e: e.tensor_scalar_mul(m0, scol[:, :], 1.0 / D), statdeps, [p+"m0"])
    V(lambda e: e.tensor_scalar_mul(t2, qcol[:, :], 1.0 / D), statdeps, [p+"t2"])
    V(lambda e: e.tensor_mul(rstd0, m0, m0), [p+"m0"], [p+"msq"])
    V(lambda e: e.tensor_sub(t2, t2, rstd0), [p+"t2", p+"msq"], [p+"v0"])
    V(lambda e: e.tensor_scalar_add(t2, t2, 1e-5), [p+"v0"], [p+"v0e"])
    A(lambda e: e.activation(rstd0, t2, AF.Ln), [p+"v0e"], [p+"ln"])
    A(lambda e: e.activation(rstd0, rstd0, AF.Exp, scale=-0.5), [p+"ln"], [p+"rstd0"])
    V(lambda e: e.tensor_mul(b00, m0, rstd0), [p+"m0", p+"rstd0"], [p+"b00a"])
    c.emit("vector", lambda e: e.tensor_scalar_mul(b00, b00, -1.0),
           reads=[p+"b00a", p+"rstd0"], writes=[pfx + "cols"], small=True)

    for tt in range(NT):
        sl = xb[:, (tt % 2) * D:(tt % 2) * D + D]
        xres = pfx + "xs%d" % (tt % 2)
        c.emit("sync", lambda e, s=sl, tt=tt: e.dma_start(
            out=s, in_=xin[tt * 128:(tt + 1) * 128, :]),
            writes=[xres], dma_slot="x%d" % (tt % 2))
        c.emit("scalar", lambda e, s=sl, tt=tt: e.activation(
            zn[:], s, AF.Identity, scale=rstd0[:, tt:tt + 1],
            bias=b00[:, tt:tt + 1]),
            reads=[xres, pfx + "cols"], writes=[pfx + "zn"])
        for cc in range(ND):
            bank = cc % 2
            c.emit("tensor", lambda e, cc=cc, b=bank: e.transpose(
                c.ps[b][0:128, 0:128], zn[:, cc * 128:(cc + 1) * 128],
                c.ident[:]),
                reads=[pfx + "zn", "consts"], writes=["psB%d" % bank])
            c.emit("vector", lambda e, cc=cc, tt=tt, b=bank: e.tensor_copy(
                hnc(cc)[:, tt * 128:(tt + 1) * 128], c.ps[b][0:128, 0:128]),
                reads=["psB%d" % bank], writes=[pfx + "hn%d" % cc])
    hnres = [pfx + "hn%d" % ccc for ccc in range(ND)]

    def wload_slab(col0):
        c.emit("gpsimd", lambda e: e.dma_start(
            out=wnat.rearrange("p (c m) -> p c m", m=D),
            in_=wd["Wqkv"][:, col0:col0 + D].rearrange("(c p) m -> p c m", p=128)),
            writes=[pfx + "wnat"], dma_slot="wload")

    def qk_tok(tt, dst_scr, scr_res):
        for oc in range(2):
            b = 2 + oc
            for mc in range(ND):
                c.emit("tensor", lambda e, tt=tt, oc=oc, mc=mc, b=b: e.matmul(
                    c.ps[b][:, :],
                    hnc(mc)[:, tt * 128:(tt + 1) * 128],
                    wnat[:, mc * D + oc * 512: mc * D + oc * 512 + 512],
                    start=(mc == 0), stop=(mc == ND - 1)),
                    reads=[pfx + "wnat", hnres[mc]], writes=["psB%d" % b])
            c.emit("vector", lambda e, oc=oc, b=b: e.tensor_copy(
                dst_scr[:, oc * 512:(oc + 1) * 512], c.ps[b][:, :]),
                reads=["psB%d" % b], writes=[scr_res])

    def feat_store(tt, src, srcres, dst, rfmt):
        for cc in range(ND):
            b = cc % 2
            c.emit("tensor", lambda e, cc=cc, b=b, src=src: e.transpose(
                c.ps[b][0:128, 0:128], src[:, cc * 128:(cc + 1) * 128],
                c.ident[:]),
                reads=[srcres, "consts"], writes=["psB%d" % b])
            c.emit("vector", lambda e, cc=cc, tt=tt, b=b, dst=dst: e.tensor_copy(
                dst[:, cc * T + tt * 128: cc * T + tt * 128 + 128],
                c.ps[b][0:128, 0:128]),
                reads=["psB%d" % b], writes=[rfmt % (cc, tt // 4)])

    wload_slab(0)
    for tt in range(NT):
        qk_tok(tt, qtile, pfx + "qt")
        feat_store(tt, qtile, pfx + "qt", qTu, pfx + "qTu_%d_%d")
    wload_slab(D)
    for tt in range(NT):
        qk_tok(tt, qtile, pfx + "qt")
        feat_store(tt, qtile, pfx + "qt", kTu, pfx + "kTu_%d_%d")

    for h in range(H):
        ch = h // 2
        pb_ = (h % 2) * 64
        for qt in range(NT):
            for kc in range(4):
                b = kc % 2
                c.emit("tensor", lambda e, ch=ch, pb_=pb_, qt=qt, kc=kc, b=b:
                       e.matmul(c.ps[b][:, :],
                                qTu[pb_:pb_ + 64,
                                    ch * T + qt * 128: ch * T + qt * 128 + 128],
                                kTu[pb_:pb_ + 64,
                                    ch * T + kc * 512: ch * T + kc * 512 + 512],
                                start=True, stop=True),
                       reads=[pfx + "qTu_%d_%d" % (ch, qt // 4),
                              pfx + "kTu_%d_%d" % (ch, kc)],
                       writes=["psB%d" % b])
                c.emit("scalar", lambda e, kc=kc, b=b: e.activation(
                    junk[:], c.ps[b][:, :], AF.Exp, scale=SC,
                    accum_out=denp[:, kc:kc + 1]),
                    reads=["psB%d" % b], writes=[pfx + "junk", pfx + "dp%d" % kc])
            c.emit("vector", lambda e, h=h, qt=qt: e.reduce_sum(
                den[:, h * NT + qt: h * NT + qt + 1], denp[:], axis=AX.X),
                reads=[pfx + "dp%d" % k_ for k_ in range(4)],
                writes=[pfx + "den_%d_%d" % (h, qt)], small=True)

    c.dump(pfx + "hn", hn[:], hnres)
    c.dump(pfx + "qTu", qTu[:], [pfx + "qTu_%d_%d" % (c_, q_)
                                 for c_ in range(ND) for q_ in range(4)])
    c.dump(pfx + "kTu", kTu[:], [pfx + "kTu_%d_%d" % (c_, q_)
                                 for c_ in range(ND) for q_ in range(4)])
    c.dump(pfx + "den", den[:], [pfx + "den_%d_%d" % (h_, q_)
                                 for h_ in range(H) for q_ in range(NT)])

    fence_w = ([pfx + "qTr_%d_%d" % (cc, qq) for cc in range(ND) for qq in range(4)]
               + [pfx + "kTr_%d_%d" % (cc, qq) for cc in range(ND) for qq in range(4)])
    fence_r = ([pfx + "qTu_%d_%d" % (cc, qq) for cc in range(ND) for qq in range(4)]
               + [pfx + "kTu_%d_%d" % (cc, qq) for cc in range(ND) for qq in range(4)])
    c.emit("gpsimd", lambda e: e.nop(), reads=fence_r, writes=fence_w)

    wload_slab(0)
    for tt in range(NT):
        qk_tok(tt, qtile, pfx + "qt")
        ct = costabs[:, tt * nf:(tt + 1) * nf]
        st_ = sintabs[:, tt * nf:(tt + 1) * nf]
        tabres = [pfx + "c%d" % tt, pfx + "s%d" % tt]
        emit_rope_apply(c, pfx + "rq%d" % tt, qtile, H, dh, ct, st_, tabres,
                        qrot, pfx + "qt", pfx + "qrot", (ra, rb))
        feat_store(tt, qrot, pfx + "qrot", qTr, pfx + "qTr_%d_%d")
    wload_slab(D)
    for tt in range(NT):
        qk_tok(tt, qtile, pfx + "qt")
        ct = costabs[:, tt * nf:(tt + 1) * nf]
        st_ = sintabs[:, tt * nf:(tt + 1) * nf]
        tabres = [pfx + "c%d" % tt, pfx + "s%d" % tt]
        emit_rope_apply(c, pfx + "rk%d" % tt, qtile, H, dh, ct, st_, tabres,
                        qrot, pfx + "qt", pfx + "qrot", (ra, rb))
        feat_store(tt, qrot, pfx + "qrot", kTr, pfx + "kTr_%d_%d")

    wload_slab(2 * D)
    for tt in range(NT):
        for oc in range(2):
            b = 2 + oc
            for mc in range(ND):
                c.emit("tensor", lambda e, tt=tt, oc=oc, mc=mc, b=b: e.matmul(
                    c.ps[b][:, :],
                    hnc(mc)[:, tt * 128:(tt + 1) * 128],
                    wnat[:, mc * D + oc * 512: mc * D + oc * 512 + 512],
                    start=(mc == 0), stop=(mc == ND - 1)),
                    reads=[pfx + "wnat", hnres[mc]], writes=["psB%d" % b])
            c.emit("vector", lambda e, tt=tt, oc=oc, b=b: e.tensor_copy(
                vtok[:, tt * D + oc * 512: tt * D + oc * 512 + 512],
                c.ps[b][:, :]),
                reads=["psB%d" % b], writes=[pfx + "v%d" % tt])

    c.dump(pfx + "qTr", qTr[:], [pfx + "qTr_%d_%d" % (c_, q_)
                                 for c_ in range(ND) for q_ in range(4)])
    c.dump(pfx + "kTr", kTr[:], [pfx + "kTr_%d_%d" % (c_, q_)
                                 for c_ in range(ND) for q_ in range(4)])
    c.dump(pfx + "vtok", vtok[:], [pfx + "v%d" % t_ for t_ in range(NT)])

    c.emit("gpsimd", lambda e: e.nop(),
           reads=hnres,
           writes=[pfx + "yTn_%d_%d" % (cc, qq)
                   for cc in range(ND) for qq in range(4)])

    for h in range(H):
        ch = h // 2
        pb_ = (h % 2) * 64
        for qc in range(4):
            qres = pfx + "qTr_%d_%d" % (ch, qc)
            kts = list(range(0, qc * 4 + 4))
            for i, kt in enumerate(kts):
                sb_ = kt % 2
                er = kt % 3
                c.emit("tensor", lambda e, ch=ch, pb_=pb_, qc=qc, kt=kt, b=sb_:
                       e.matmul(c.ps[b][:, :],
                                kTr[pb_:pb_ + 64,
                                    ch * T + kt * 128: ch * T + kt * 128 + 128],
                                qTr[pb_:pb_ + 64,
                                    ch * T + qc * 512: ch * T + qc * 512 + 512],
                                start=True, stop=True),
                       reads=[pfx + "kTr_%d_%d" % (ch, kt // 4), qres],
                       writes=["psB%d" % sb_])
                c.emit("scalar", lambda e, b=sb_, er=er: e.activation(
                    expb[:, er * 512:(er + 1) * 512], c.ps[b][:, :],
                    AF.Exp, scale=SC),
                    reads=["psB%d" % sb_], writes=["exp%d" % er])
                kb0 = kt * 128
                qb0 = qc * 512
                if kb0 + 127 >= qb0:
                    c.emit("gpsimd", lambda e, er=er, off=qb0 - kb0: e.affine_select(
                        out=expb[:, er * 512:(er + 1) * 512],
                        in_=expb[:, er * 512:(er + 1) * 512],
                        compare_op=ALU.is_ge, fill=0.0, base=off,
                        pattern=[[1, 512]], channel_multiplier=-1),
                        reads=["exp%d" % er], writes=["exp%d" % er])
                c.emit("tensor", lambda e, h=h, kt=kt, er=er, i=i, n=len(kts):
                       e.matmul(c.ps[2][0:64, :],
                                vtok[:, kt * D + h * 64: kt * D + h * 64 + 64],
                                expb[:, er * 512:(er + 1) * 512],
                                start=(i == 0), stop=(i == n - 1)),
                       reads=["exp%d" % er, pfx + "v%d" % kt],
                       writes=["psB2"])
            for j in range(4):
                c.emit("tensor", lambda e, h=h, qc=qc, j=j: e.transpose(
                    c.ps[3][0:1, j * 128:(j + 1) * 128],
                    den[:, h * NT + qc * 4 + j: h * NT + qc * 4 + j + 1],
                    c.ident[:]),
                    reads=[pfx + "den_%d_%d" % (h, qc * 4 + j), "consts"],
                    writes=["psB3"])
                c.emit("vector", lambda e, j=j: e.tensor_copy(
                    denrow[0:1, j * 128:(j + 1) * 128],
                    c.ps[3][0:1, j * 128:(j + 1) * 128]),
                    reads=["psB3"], writes=[pfx + "dr"], small=True)
            c.emit("scalar", lambda e: e.activation(
                denrec[:], denrow[:], AF.Ln),
                reads=[pfx + "dr"], writes=[pfx + "drr"])
            c.emit("scalar", lambda e: e.activation(
                denrec[:], denrec[:], AF.Exp, scale=-1.0),
                reads=[pfx + "drr"], writes=[pfx + "drr"])
            c.emit("tensor", lambda e: e.matmul(
                c.ps[4][0:64, :], c.ones_rf[0:1, 0:64], denrec[:],
                start=True, stop=True),
                reads=[pfx + "drr", "consts"], writes=["psB4"])
            c.emit("vector", lambda e: e.tensor_copy(bca[0:64, :], c.ps[4][0:64, :]),
                   reads=["psB4"], writes=[pfx + "bca"])
            c.emit("vector", lambda e, ch=ch, pb_=pb_, qc=qc: e.tensor_mul(
                yTn[pb_:pb_ + 64, ch * T + qc * 512: ch * T + qc * 512 + 512],
                c.ps[2][0:64, :], bca[0:64, :]),
                reads=["psB2", pfx + "bca"],
                writes=[pfx + "yTn_%d_%d" % (ch, qc)])

    c.dump(pfx + "yTn", yTn[:], [pfx + "yTn_%d_%d" % (c_, q_)
                                 for c_ in range(ND) for q_ in range(4)])

    c.emit("gpsimd", lambda e: e.dma_start(
        out=wnat.rearrange("p (c m) -> p c m", m=D),
        in_=wd["Wout"].rearrange("(c p) m -> p c m", p=128)),
        writes=[pfx + "wnat"], dma_slot="wload")
    for tt in range(NT):
        sl = xb[:, (tt % 2) * D:(tt % 2) * D + D]
        xres = pfx + "xs%d" % (tt % 2)
        c.emit("sync", lambda e, s=sl, tt=tt: e.dma_start(
            out=s, in_=xin[tt * 128:(tt + 1) * 128, :]),
            writes=[xres], dma_slot="x%d" % (tt % 2))
        for oc in range(2):
            b = oc
            for mc in range(ND):
                c.emit("tensor", lambda e, tt=tt, oc=oc, mc=mc, b=b: e.matmul(
                    c.ps[b][:, :],
                    yTn[:, mc * T + tt * 128: mc * T + tt * 128 + 128],
                    wnat[:, mc * D + oc * 512: mc * D + oc * 512 + 512],
                    start=(mc == 0), stop=False),
                    reads=[pfx + "yTn_%d_%d" % (mc, tt // 4), pfx + "wnat"],
                    writes=["psB%d" % b])
            c.emit("tensor", lambda e, oc=oc, b=b: e.matmul(
                c.ps[b][:, :], c.ones_r[0:1, 0:128],
                bout_r[0:1, oc * 512:(oc + 1) * 512], start=False, stop=True),
                reads=[pfx + "bout_r", "consts"], writes=["psB%d" % b])
            c.emit("vector", lambda e, s=sl, oc=oc, b=b: e.tensor_add(
                s[:, oc * 512:(oc + 1) * 512], s[:, oc * 512:(oc + 1) * 512],
                c.ps[b][:, :]),
                reads=["psB%d" % b, xres], writes=[xres])
        c.emit("sync", lambda e, s=sl, tt=tt: e.dma_start(
            out=xout[tt * 128:(tt + 1) * 128, :], in_=s),
            reads=[xres], dma_slot="st%d" % (tt % 2))


def _build_l0attn_program():
    es = contextlib.ExitStack()
    with es:
        nc = bass.Bass("TRN2", target_bir_lowering=False, detect_race_conditions=False)
        xin = nc.dram_tensor("x", [T, D], F32, kind="ExternalInput").ap()
        xout = nc.dram_tensor("xo", [T, D], F32, kind="ExternalOutput").ap()
        wd = {}
        for k, shp in [("Wqkv", [D, 3 * D]), ("Wout", [D, D]), ("bout", [D])]:
            wd[k] = nc.dram_tensor(k, shp, F32, kind="ExternalInput").ap()
        c = Core(nc, es)
        emit_l0attn(c, xin, xout, wd)
        c.end_barrier()
        stats = c.finalize()
    return nc, stats


def run_l0attn(x_full, w):
    nc, stats = _build_l0attn_program()
    in_maps = []
    for b in range(8):
        m = {"x": np.ascontiguousarray(x_full[b])}
        m.update(w)
        in_maps.append(m)
    r = run_bass_kernel_spmd(nc, in_maps, core_ids=list(range(8)))
    return np.stack([r.results[b]["xo"] for b in range(8)])


def emit_xpos_apply(c, pfx, qtile, nheads, dh, Cq, Sq, tabres, out,
                    inres, outres, scr, scrres=None):
    nf = dh // 2
    a, b = scr
    AR = (scrres or pfx) + "a"
    BR = (scrres or pfx) + "b"

    def mk(stage):
        def g(h):
            lo = slice(h * dh, h * dh + nf)
            hi = slice(h * dh + nf, h * dh + dh)
            sa = slice(h * nf, (h + 1) * nf)
            if stage == 0:
                return lambda e: e.tensor_mul(a[:, sa], qtile[:, lo], Cq[:, 0:nf])
            if stage == 1:
                return lambda e: e.tensor_mul(b[:, sa], qtile[:, hi], Sq[:, 0:nf])
            if stage == 2:
                return lambda e: e.tensor_sub(out[:, lo], a[:, sa], b[:, sa])
            if stage == 3:
                return lambda e: e.tensor_mul(a[:, sa], qtile[:, hi], Cq[:, nf:dh])
            if stage == 4:
                return lambda e: e.tensor_mul(b[:, sa], qtile[:, lo], Sq[:, nf:dh])
            return lambda e: e.tensor_add(out[:, hi], a[:, sa], b[:, sa])
        return g

    plan = ((0, [inres] + tabres, [AR]), (1, [inres] + tabres, [BR]),
            (2, [AR, BR], [outres]), (3, [inres] + tabres, [AR]),
            (4, [inres] + tabres, [BR]), (5, [AR, BR], [outres]))
    for stage, rds, wrs in plan:
        g = mk(stage)
        for h in range(nheads):
            c.emit("vector", g(h), reads=rds, writes=wrs,
                   small=(nheads == 1))


def emit_l0ffn(c: Core, xin, xout, wd, pfx="l0f"):
    nc = c.nc
    H, dh = 8, 64
    nf = 32
    HID = 2730
    TD = T + 1

    fb = c.sb(pfx + "fb", [128, nf], F32)
    emit_const_freq(c, pfx + "gf", fb, freq_bits_recip(nf, 1e4, 32))
    c.emit("vector", lambda e: e.engine_nop(), reads=[pfx + "gffb"],
           writes=[pfx + "freq"], small=True)
    c32 = c.sb(pfx + "c32", [128, NT * nf], F32)
    s32 = c.sb(pfx + "s32", [128, NT * nf], F32)
    emit_rope_tables(c, pfx + "tb", fb, nf, c32, s32, pfx)

    brow = c.sb(pfx + "brow", [128, nf], F32)
    tcol2 = c.sb(pfx + "tc2", [128, 1], F32)
    exs = c.sb(pfx + "exs", [128, nf], F32)
    sc32 = c.sb(pfx + "sc32", [128, NT * nf], F32)
    isc32 = c.sb(pfx + "isc32", [128, NT * nf], F32)
    bb = ((np.arange(0, 2 * nf, 2, dtype=np.float32) + np.float32(25.6))
          / np.float32(89.6))
    for j in range(nf):
        c.emit("gpsimd", lambda e, j=j, v=float(bb[j]):
               e.memset(brow[:, j:j + 1], v),
               writes=[pfx + "brow"], small=True)
    c.emit("scalar", lambda e: e.activation(brow[:], brow[:], AF.Ln),
           reads=[pfx + "brow"], writes=[pfx + "lnb"], small=True)
    for tt in range(NT):
        c.emit("gpsimd", lambda e, tt=tt: e.iota(
            tcol2[:], pattern=[[0, 1]], base=tt * 128, channel_multiplier=1,
            allow_small_or_imprecise_dtypes=True),
            writes=[pfx + "tc2"], small=True)
        c.emit("vector", lambda e: e.tensor_scalar(
            tcol2[:], tcol2[:], 1.0 / 256.0, -4.0, op0=ALU.mult, op1=ALU.add),
            reads=[pfx + "tc2"], writes=[pfx + "tc2"], small=True)
        c.emit("vector", lambda e: e.tensor_scalar(
            exs[:], brow[:], tcol2[0:128, 0:1], None, op0=ALU.mult),
            reads=[pfx + "tc2", pfx + "lnb"], writes=[pfx + "exs"], small=True)
        c.emit("scalar", lambda e, tt=tt: e.activation(
            sc32[:, tt * nf:(tt + 1) * nf], exs[:], AF.Exp),
            reads=[pfx + "exs"], writes=[pfx + "sc%d" % tt], small=True)
        c.emit("scalar", lambda e, tt=tt: e.activation(
            isc32[:, tt * nf:(tt + 1) * nf], exs[:], AF.Exp, scale=-1.0),
            reads=[pfx + "exs"], writes=[pfx + "isc%d" % tt], small=True)
    tabdeps = ([pfx + "c%d" % t for t in range(NT)] +
               [pfx + "s%d" % t for t in range(NT)] +
               [pfx + "sc%d" % t for t in range(NT)] +
               [pfx + "isc%d" % t for t in range(NT)])

    xb = c.sb(pfx + "x", [128, D], F32)
    scol = c.sb(pfx + "S", [128, 4 * NT], F32)
    shcol = c.sb(pfx + "Sh", [128, 2 * NT], F32)
    sqs = c.sb(pfx + "sq", [128, D], BF16)
    stat = c.sb(pfx + "stat", [128, 8 * NT], F32)
    z = c.sb(pfx + "z", [128, NT * D], F32)
    zf = c.sb(pfx + "zf", [128, ND * TD], BF16)
    wnat = c.sb(pfx + "wnat", [128, ND * 512], BF16)
    arena = c.sb(pfx + "arena", [128, 84736], mybir.dt.uint8)
    qTr = c.sb_overlay(pfx + "qTrT", [128, 4 * T], BF16, arena, 0)
    kTr = c.sb_overlay(pfx + "kTrT", [128, 4 * T], BF16, arena, 16384)
    vtok = c.sb_overlay(pfx + "vT", [128, NT * (H * 65)], BF16, arena, 32768)
    ytn = c.sb_overlay(pfx + "ytnT", [128, 4 * T], BF16, arena, 49408)
    wonat = c.sb_overlay(pfx + "woT", [128, 4 * D], BF16, arena, 65792)
    wnat = wonat
    h2n = c.sb_overlay(pfx + "h2nT", [128, ND * 512], BF16, arena, 0)
    rstdb = c.sb_overlay(pfx + "rstdbT", [128, 512], F32, arena, 8192)
    mub = c.sb_overlay(pfx + "mubT", [128, 512], F32, arena, 10240)
    w1blk = c.sb_overlay(pfx + "w1bT", [128, 2 * ND * 256], BF16, arena, 12288)
    w2hblk = c.sb_overlay(pfx + "w2hT", [128, 22 * 512], BF16, arena, 20480)
    h3 = c.sb_overlay(pfx + "h3T", [128, 22 * 512], BF16, arena, 43008)
    b1r = c.sb_overlay(pfx + "b1rT", [1, 5504], BF16, arena, 65536)
    b2r = c.sb_overlay(pfx + "b2rT", [1, D], BF16, arena, 76544)
    Cts = c.sb(pfx + "Cts", [128, dh], F32)
    Sts = c.sb(pfx + "Sts", [128, dh], F32)
    qsb = c.sb(pfx + "qsb", [128, dh], F32)
    ksb = c.sb(pfx + "ksb", [128, dh], F32)
    qtile = c.sb_overlay(pfx + "qtT", [128, 512], F32, arena, 78592)
    qrot = c.sb_overlay(pfx + "qrT", [128, 512], F32, arena, 80640)
    ra = c.sb_overlay(pfx + "raT", [128, H * nf], F32, arena, 76032)
    rb = c.sb_overlay(pfx + "rbT", [128, H * nf], F32, arena, 77056)
    sscol = c.sb(pfx + "ss", [128, H], F32)
    expb = c.sb_overlay(pfx + "expT", [128, 2 * 512], BF16, arena, 73984)
    denrec = c.sb_overlay(pfx + "drrT", [1, 512], F32, arena, 78592)
    bca = c.sb_overlay(pfx + "bcaT", [128, 512], F32, arena, 80640)
    geb = c.sb_overlay(pfx + "gebT", [128, 512], F32, arena, 78592)
    rrow = c.sb_overlay(pfx + "rrowT", [1, 512], F32, arena, 80640)
    vctmp = c.sb(pfx + "vctmp", [ND, 128], F32)
    mrow = c.sb_overlay(pfx + "mrowT", [1, 512], F32, arena, 82688)

    AREG = ([pfx + "qTr_%d_%d" % (cc, q_) for cc in range(4) for q_ in range(4)]
            + [pfx + "kTr_%d_%d" % (cc, q_) for cc in range(4) for q_ in range(4)]
            + [pfx + "v%d" % t_ for t_ in range(NT)] + [pfx + "vones"]
            + [pfx + "ytn_%d_%d" % (cc, q_) for cc in range(4) for q_ in range(4)]
            + [pfx + "wnat", "exp0", "exp1", "exp2",
               pfx + "drr", pfx + "bca", pfx + "qt", pfx + "qrot",
               pfx + "ropescra", pfx + "ropescrb", pfx + "Cts"])
    BREG = ([pfx + "h2n%d" % c_ for c_ in range(ND)]
            + [pfx + "h3_%d" % j_ for j_ in range(22)]
            + [pfx + "w1a0", pfx + "w1a1", pfx + "w1g0", pfx + "w1g1",
               pfx + "w2h", pfx + "rstdb", pfx + "mub",
               pfx + "b1r", pfx + "b2r", pfx + "geb",
               pfx + "rr", pfx + "mr", pfx + "gel"])

    def zfc(cc):
        return zf[:, cc * TD:(cc + 1) * TD]

    for tt in range(NT):
        sl = xb[:, 0:D]
        xres = pfx + "xs"
        c.emit("sync", lambda e, s=sl, tt=tt: e.dma_start(
            out=s, in_=xin[tt * 128:(tt + 1) * 128, :]),
            writes=[xres], dma_slot="x0")
        c.emit("scalar", lambda e, s=sl, tt=tt: e.activation(
            sqs[:], s, AF.Square, accum_out=scol[:, 2 * NT + tt:2 * NT + tt + 1]),
            reads=[xres], writes=[pfx + "sqs", pfx + "Q%d" % tt])
        c.emit("vector", lambda e, s=sl, tt=tt: e.reduce_sum(
            scol[:, tt:tt + 1], s, axis=AX.X),
            reads=[xres], writes=[pfx + "S%d" % tt])
    m0 = stat[:, 0:NT]
    rstd0 = stat[:, NT:2 * NT]
    t2 = stat[:, 2 * NT:3 * NT]
    b00 = stat[:, 3 * NT:4 * NT]
    statdeps = ([pfx + "Q%d" % t for t in range(NT)] +
                [pfx + "S%d" % t for t in range(NT)])
    V = lambda fn, r, w: c.emit("vector", fn, reads=r, writes=w, small=True)
    A = lambda fn, r, w: c.emit("scalar", fn, reads=r, writes=w, small=True)
    p = pfx
    V(lambda e: e.tensor_scalar_mul(m0, scol[:, 0:NT], 1.0 / D), statdeps, [p+"m0"])
    V(lambda e: e.tensor_scalar_mul(t2, scol[:, 2 * NT:3 * NT], 1.0 / D),
      statdeps, [p+"t2"])
    V(lambda e: e.tensor_mul(rstd0, m0, m0), [p+"m0"], [p+"msq"])
    V(lambda e: e.tensor_sub(t2, t2, rstd0), [p+"t2", p+"msq"], [p+"v0"])
    V(lambda e: e.tensor_scalar_add(t2, t2, 1e-5), [p+"v0"], [p+"v0e"])
    A(lambda e: e.activation(rstd0, t2, AF.Ln), [p+"v0e"], [p+"ln"])
    A(lambda e: e.activation(rstd0, rstd0, AF.Exp, scale=-0.5), [p+"ln"], [p+"rstd0"])
    V(lambda e: e.tensor_mul(b00, m0, rstd0), [p+"m0", p+"rstd0"], [p+"b00a"])
    c.emit("vector", lambda e: e.tensor_scalar_mul(b00, b00, -1.0),
           reads=[p+"b00a", p+"rstd0"], writes=[pfx + "cols"], small=True)
    for tt in range(NT):
        sl = xb[:, 0:D]
        xres = pfx + "xs"
        c.emit("sync", lambda e, s=sl, tt=tt: e.dma_start(
            out=s, in_=xin[tt * 128:(tt + 1) * 128, :]),
            writes=[xres], dma_slot="x0")
        c.emit("scalar", lambda e, s=sl, tt=tt: e.activation(
            z[:, tt * D:(tt + 1) * D], s, AF.Identity,
            scale=rstd0[:, tt:tt + 1], bias=b00[:, tt:tt + 1]),
            reads=[xres, pfx + "cols"], writes=[pfx + "z%d" % tt])

    def zf_zero(e):
        for cc in range(ND):
            e.memset(zfc(cc)[:, 0:1], 0.0)
        return e.nop()
    c.emit("gpsimd", zf_zero, writes=[pfx + "zf0"])

    def build_zf():
        for tt in range(NT):
            for cc in range(ND):
                b = cc % 2
                c.emit("tensor", lambda e, tt=tt, cc=cc, b=b: e.transpose(
                    c.ps[b][0:128, 0:128],
                    z[:, tt * D + cc * 128: tt * D + cc * 128 + 128],
                    c.ident[:]),
                    reads=[pfx + "z%d" % tt, "consts"], writes=["psB%d" % b])
                c.emit("vector", lambda e, tt=tt, cc=cc, b=b: e.tensor_copy(
                    zfc(cc)[:, 1 + tt * 128: 1 + tt * 128 + 128],
                    c.ps[b][0:128, 0:128]),
                    reads=["psB%d" % b], writes=[pfx + "zf%d" % cc])
    zfres = [pfx + "zf%d" % cc for cc in range(ND)] + [pfx + "zf0"]

    def stat_sh_shift():
        c.emit("gpsimd", lambda e: e.memset(shcol[:], 0.0),
               writes=[pfx + "shA", pfx + "shB", pfx + "shC", pfx + "shD"])
        c.emit("gpsimd", lambda e: e.dma_start(
            out=shcol[1:128, 0:NT], in_=scol[0:127, NT:2 * NT]),
            reads=[pfx + "S2"], writes=[pfx + "shA"], dma_slot="shift")
        c.emit("gpsimd", lambda e: e.dma_start(
            out=shcol[0:1, 1:NT], in_=scol[127:128, NT:2 * NT - 1]),
            reads=[pfx + "S2"], writes=[pfx + "shB"], dma_slot="shift")
        c.emit("gpsimd", lambda e: e.dma_start(
            out=shcol[1:128, NT:2 * NT], in_=scol[0:127, 3 * NT:4 * NT]),
            reads=[pfx + "Q2"], writes=[pfx + "shC"], dma_slot="shift")
        c.emit("gpsimd", lambda e: e.dma_start(
            out=shcol[0:1, NT + 1:2 * NT], in_=scol[127:128, 3 * NT:4 * NT - 1]),
            reads=[pfx + "Q2"], writes=[pfx + "shD"], dma_slot="shift")

    c.dump(pfx + "z0", z[:], [pfx + "z%d" % t_ for t_ in range(NT)])

    for l in range(2):
        wqkv = wd["wqkv"][l]
        c.emit("gpsimd", lambda e: e.nop(), reads=BREG, writes=AREG)
        build_zf()

        for nm, src, dstb in (("qs", wd["q_scale"], qsb), ("ks", wd["k_scale"], ksb)):
            c.emit("gpsimd", lambda e, src=src, dstb=dstb, l=l: e.dma_start(
                out=dstb[0:1, :], in_=src[l:l + 1, :]),
                writes=[pfx + nm + "row"], dma_slot="vload_g")
            c.emit("tensor", lambda e, dstb=dstb: e.matmul(
                c.ps[5][:, 0:dh], c.ones_rf[:], dstb[0:1, :],
                start=True, stop=True),
                reads=[pfx + nm + "row", "consts"], writes=["psB5"])
            c.emit("vector", lambda e, dstb=dstb: e.tensor_copy(
                dstb[:], c.ps[5][:, 0:dh]),
                reads=["psB5"], writes=[pfx + nm + "b"])
        for sub, colbase in (("q", 0), ("k", 512), ("v", 1024)):
            c.emit("gpsimd", lambda e, colbase=colbase, wqkv=wqkv: e.dma_start(
                out=wnat.rearrange("p (c m) -> p c m", m=512),
                in_=wqkv[:, colbase:colbase + 512].rearrange(
                    "(c p) m -> p c m", p=128)),
                writes=[pfx + "wnat"], dma_slot="wload")
            if Core.DEBUG and l == 0 and sub == "v":
                c.dump(pfx + "wnatd", wnat[:], [pfx + "wnat"])
            for tt in range(NT):
                b = 2 + (tt % 2)
                for mc in range(ND):
                    shift = mc >= 4
                    c.emit("tensor", lambda e, tt=tt, mc=mc, b=b, sh=shift:
                           e.matmul(c.ps[b][:, :],
                                    zfc(mc)[:, (0 if sh else 1) + tt * 128:
                                            (0 if sh else 1) + tt * 128 + 128],
                                    wnat[:, mc * 512:(mc + 1) * 512],
                                    start=(mc == 0), stop=(mc == ND - 1)),
                           reads=[pfx + "wnat", pfx + "zf%d" % mc, pfx + "zf0"],
                           writes=["psB%d" % b])
                if sub == "v":
                    c.emit("vector", lambda e, tt=tt, b=b: e.tensor_copy(
                        vtok[:, tt * H * 65: tt * H * 65 + H * 65].rearrange(
                            "p (h n) -> p h n", n=65)[:, :, 0:64],
                        c.ps[b][:, :].rearrange("p (h n) -> p h n", n=64)),
                        reads=["psB%d" % b], writes=[pfx + "v%d" % tt])
                    continue
                c.emit("vector", lambda e, b=b: e.tensor_copy(
                    qtile[:], c.ps[b][:, :]),
                    reads=["psB%d" % b], writes=[pfx + "qt"])
                for h in range(H):
                    c.emit("scalar", lambda e, h=h: e.activation(
                        sqs[:, 0:dh], qtile[:, h * dh:(h + 1) * dh], AF.Square,
                        accum_out=sscol[:, h:h + 1]),
                        reads=[pfx + "qt"], writes=[pfx + "sqs", pfx + "ssc"])
                A(lambda e: e.activation(sscol[:], sscol[:], AF.Ln),
                  [pfx + "ssc"], [pfx + "ssc"])
                A(lambda e: e.activation(sscol[:], sscol[:], AF.Exp, scale=-0.5),
                  [pfx + "ssc"], [pfx + "ssc"])
                V(lambda e: e.tensor_scalar_min(sscol[:], sscol[:], 1e12),
                  [pfx + "ssc"], [pfx + "ssc"])
                for h in range(H):
                    c.emit("scalar", lambda e, h=h: e.activation(
                        qtile[:, h * dh:(h + 1) * dh],
                        qtile[:, h * dh:(h + 1) * dh],
                        AF.Identity, scale=sscol[:, h:h + 1]),
                        reads=[pfx + "ssc", pfx + "qt"], writes=[pfx + "qt"])
                scb = sc32 if sub == "q" else isc32
                ssel = qsb if sub == "q" else ksb
                tres = pfx + "Cts"
                tsl = slice(tt * nf, (tt + 1) * nf)
                c.emit("vector", lambda e, tsl=tsl, scb=scb: e.tensor_mul(
                    Cts[:, 0:nf], c32[:, tsl], scb[:, tsl]),
                    reads=tabdeps, writes=[tres], small=True)
                c.emit("vector", lambda e, tsl=tsl, scb=scb: e.tensor_mul(
                    Cts[:, nf:dh], c32[:, tsl], scb[:, tsl]),
                    reads=tabdeps, writes=[tres], small=True)
                c.emit("vector", lambda e, tsl=tsl, scb=scb: e.tensor_mul(
                    Sts[:, 0:nf], s32[:, tsl], scb[:, tsl]),
                    reads=tabdeps, writes=[tres], small=True)
                c.emit("vector", lambda e, tsl=tsl, scb=scb: e.tensor_mul(
                    Sts[:, nf:dh], s32[:, tsl], scb[:, tsl]),
                    reads=tabdeps, writes=[tres], small=True)
                c.emit("vector", lambda e, ssel=ssel: e.tensor_mul(
                    Cts[:, 0:nf], Cts[:, 0:nf], ssel[:, 0:nf]),
                    reads=[tres, pfx + "qsb", pfx + "ksb"], writes=[tres],
                    small=True)
                c.emit("vector", lambda e, ssel=ssel: e.tensor_mul(
                    Cts[:, nf:dh], Cts[:, nf:dh], ssel[:, nf:dh]),
                    reads=[tres], writes=[tres], small=True)
                c.emit("vector", lambda e, ssel=ssel: e.tensor_mul(
                    Sts[:, 0:nf], Sts[:, 0:nf], ssel[:, nf:dh]),
                    reads=[tres], writes=[tres], small=True)
                c.emit("vector", lambda e, ssel=ssel: e.tensor_mul(
                    Sts[:, nf:dh], Sts[:, nf:dh], ssel[:, 0:nf]),
                    reads=[tres], writes=[tres], small=True)
                emit_xpos_apply(c, pfx + "r%s%d" % (sub, tt), qtile, H, dh,
                                Cts, Sts, [tres],
                                qrot, pfx + "qt", pfx + "qrot", (ra, rb),
                                scrres=pfx + "ropescr")
                if Core.DEBUG and l == 0 and sub == "q" and tt == 0:
                    snapK = c.sb_overlay(pfx + "snapK", [128, 4 * 512], F32, arena, 20480)
                    c.emit("vector", lambda e: e.tensor_copy(
                        snapK[:, 0:512], qtile[:]), reads=[pfx + "qt"],
                        writes=[pfx + "snapK"])
                    c.emit("vector", lambda e: e.tensor_copy(
                        snapK[:, 512:1024], qrot[:]), reads=[pfx + "qrot"],
                        writes=[pfx + "snapK"])
                    c.emit("vector", lambda e: e.tensor_copy(
                        snapK[:, 1024:1088], Cts[:]), reads=[pfx + "Cts"],
                        writes=[pfx + "snapK"])
                    c.emit("vector", lambda e: e.tensor_copy(
                        snapK[:, 1088:1152], Sts[:]), reads=[pfx + "Cts"],
                        writes=[pfx + "snapK"])
                    c.emit("vector", lambda e: e.tensor_copy(
                        snapK[:, 1152:1160], sscol[:]), reads=[pfx + "ssc"],
                        writes=[pfx + "snapK"])
                    c.dump(pfx + "snapKd", snapK[:], [pfx + "snapK"])
                dstT = qTr if sub == "q" else kTr
                for cc in range(4):
                    b2_ = cc % 2
                    c.emit("tensor", lambda e, cc=cc, b=b2_: e.transpose(
                        c.ps[b][0:128, 0:128], qrot[:, cc * 128:(cc + 1) * 128],
                        c.ident[:]),
                        reads=[pfx + "qrot", "consts"], writes=["psB%d" % b2_])
                    c.emit("vector", lambda e, cc=cc, tt=tt, b=b2_, dstT=dstT,
                           sub=sub: e.tensor_copy(
                        dstT[:, cc * T + tt * 128: cc * T + tt * 128 + 128],
                        c.ps[b][0:128, 0:128]),
                        reads=["psB%d" % b2_],
                        writes=[pfx + ("qTr" if sub == "q" else "kTr")
                                + "_%d_%d" % (cc, tt // 4)])
        def vones(e):
            for tt in range(NT):
                e.memset(vtok[:, tt * H * 65: (tt + 1) * H * 65].rearrange(
                    "p (h n) -> p h n", n=65)[:, :, 64:65], 1.0)
            return e.nop()
        c.emit("gpsimd", vones, writes=[pfx + "vones"])

        c.emit("gpsimd", lambda e: e.nop(),
               reads=[pfx + "qt", pfx + "qrot", pfx + "ropescra",
                      pfx + "ropescrb"],
               writes=[pfx + "drr", pfx + "bca"])

        for h in range(H):
            ch = h // 2
            pb_ = (h % 2) * 64
            for w in range(4):
                kts = list(range(max(0, w * 4 - 4), w * 4 + 4))
                qres = pfx + "qTr_%d_%d" % (ch, w)
                for i, kt in enumerate(kts):
                    sb_ = kt % 2
                    er = kt % 2
                    c.emit("tensor", lambda e, ch=ch, pb_=pb_, w=w, kt=kt, b=sb_:
                           e.matmul(c.ps[b][:, :],
                                    kTr[pb_:pb_ + 64,
                                        ch * T + kt * 128: ch * T + kt * 128 + 128],
                                    qTr[pb_:pb_ + 64,
                                        ch * T + w * 512: ch * T + w * 512 + 512],
                                    start=True, stop=True),
                           reads=[pfx + "kTr_%d_%d" % (ch, kt // 4), qres],
                           writes=["psB%d" % sb_])
                    c.emit("scalar", lambda e, b=sb_, er=er: e.activation(
                        expb[:, er * 512:(er + 1) * 512], c.ps[b][:, :],
                        AF.Exp, scale=8.0),
                        reads=["psB%d" % sb_], writes=["exp%d" % er])
                    kb0 = kt * 128
                    qb0 = w * 512
                    if kb0 >= qb0:
                        c.emit("gpsimd", lambda e, er=er, off=qb0 - kb0:
                               e.affine_select(
                                   out=expb[:, er * 512:(er + 1) * 512],
                                   in_=expb[:, er * 512:(er + 1) * 512],
                                   compare_op=ALU.is_ge, fill=0.0, base=off,
                                   pattern=[[1, 512]], channel_multiplier=-1),
                               reads=["exp%d" % er], writes=["exp%d" % er])
                    else:
                        c.emit("gpsimd", lambda e, er=er, off=512 - qb0 + kb0:
                               e.affine_select(
                                   out=expb[:, er * 512:(er + 1) * 512],
                                   in_=expb[:, er * 512:(er + 1) * 512],
                                   compare_op=ALU.is_ge, fill=0.0, base=off,
                                   pattern=[[-1, 512]], channel_multiplier=1),
                               reads=["exp%d" % er], writes=["exp%d" % er])
                    c.emit("tensor", lambda e, h=h, kt=kt, er=er, i=i,
                           n=len(kts): e.matmul(
                               c.ps[2][0:65, :],
                               vtok[:, kt * H * 65 + h * 65:
                                    kt * H * 65 + h * 65 + 65],
                               expb[:, er * 512:(er + 1) * 512],
                               start=(i == 0), stop=(i == n - 1)),
                           reads=["exp%d" % er, pfx + "v%d" % kt,
                                  pfx + "vones"],
                           writes=["psB2"])
                c.emit("scalar", lambda e: e.activation(
                    denrec[:], c.ps[2][64:65, :], AF.Ln),
                    reads=["psB2"], writes=[pfx + "drr"])
                c.emit("scalar", lambda e: e.activation(
                    denrec[:], denrec[:], AF.Exp, scale=-1.0),
                    reads=[pfx + "drr"], writes=[pfx + "drr"])
                c.emit("tensor", lambda e: e.matmul(
                    c.ps[4][0:64, :], c.ones_rf[0:1, 0:64], denrec[:],
                    start=True, stop=True),
                    reads=[pfx + "drr", "consts"], writes=["psB4"])
                c.emit("vector", lambda e: e.tensor_copy(
                    bca[0:64, :], c.ps[4][0:64, :]),
                    reads=["psB4"], writes=[pfx + "bca"])
                c.emit("vector", lambda e, ch=ch, pb_=pb_, w=w: e.tensor_mul(
                    ytn[pb_:pb_ + 64, ch * T + w * 512: ch * T + w * 512 + 512],
                    c.ps[2][0:64, :], bca[0:64, :]),
                    reads=["psB2", pfx + "bca"],
                    writes=[pfx + "ytn_%d_%d" % (ch, w)])

        if Core.DEBUG and l == 0:
            c.dump(pfx + "zfd", zf[:], zfres)
            c.dump(pfx + "qTr0", qTr[:], [pfx + "qTr_%d_%d" % (c_, q_)
                                          for c_ in range(4) for q_ in range(4)])
            c.dump(pfx + "kTr0", kTr[:], [pfx + "kTr_%d_%d" % (c_, q_)
                                          for c_ in range(4) for q_ in range(4)])
            c.dump(pfx + "v0", vtok[:], [pfx + "v%d" % t_ for t_ in range(NT)]
                   + [pfx + "vones"])
            c.dump(pfx + "ytn0", ytn[:], [pfx + "ytn_%d_%d" % (c_, q_)
                                          for c_ in range(4) for q_ in range(4)])

        c.emit("gpsimd", lambda e, l=l: e.dma_start(
            out=wonat.rearrange("p (c m) -> p c m", m=D),
            in_=wd["wo"][l].rearrange("(c p) m -> p c m", p=128)),
            writes=[pfx + "wnat"], dma_slot="wload")
        for tt in range(NT):
            for oc in range(2):
                b = oc
                for mc in range(4):
                    c.emit("tensor", lambda e, tt=tt, oc=oc, mc=mc, b=b:
                           e.matmul(c.ps[b][:, :],
                                    ytn[:, mc * T + tt * 128:
                                        mc * T + tt * 128 + 128],
                                    wonat[:, mc * D + oc * 512:
                                          mc * D + oc * 512 + 512],
                                    start=(mc == 0), stop=(mc == 3)),
                           reads=[pfx + "ytn_%d_%d" % (mc, tt // 4), pfx + "wnat"],
                           writes=["psB%d" % b])
                c.emit("vector", lambda e, tt=tt, oc=oc, b=b: e.tensor_add(
                    z[:, tt * D + oc * 512: tt * D + oc * 512 + 512],
                    z[:, tt * D + oc * 512: tt * D + oc * 512 + 512],
                    c.ps[b][:, :]),
                    reads=["psB%d" % b, pfx + "z%d" % tt],
                    writes=[pfx + "z%d" % tt])

        for tt in range(NT):
            c.emit("scalar", lambda e, tt=tt: e.activation(
                sqs[:, 0:512], z[:, tt * D: tt * D + 512], AF.Square,
                accum_out=scol[:, 2 * NT + tt:2 * NT + tt + 1]),
                reads=[pfx + "z%d" % tt],
                writes=[pfx + "sqs", pfx + "Q1%d" % tt, pfx + "Q%d" % tt])
            c.emit("scalar", lambda e, tt=tt: e.activation(
                sqs[:, 0:512], z[:, tt * D + 512: tt * D + 1024], AF.Square,
                accum_out=scol[:, 3 * NT + tt:3 * NT + tt + 1]),
                reads=[pfx + "z%d" % tt], writes=[pfx + "sqs", pfx + "Q2c%d" % tt])
            c.emit("vector", lambda e, tt=tt: e.reduce_sum(
                scol[:, tt:tt + 1], z[:, tt * D: tt * D + 512], axis=AX.X),
                reads=[pfx + "z%d" % tt],
                writes=[pfx + "S1%d" % tt, pfx + "S%d" % tt])
            c.emit("vector", lambda e, tt=tt: e.reduce_sum(
                scol[:, NT + tt:NT + tt + 1],
                z[:, tt * D + 512: tt * D + 1024], axis=AX.X),
                reads=[pfx + "z%d" % tt], writes=[pfx + "S2c%d" % tt])
        c.emit("vector", lambda e: e.engine_nop(),
               reads=[pfx + "S2c%d" % t_ for t_ in range(NT)],
               writes=[pfx + "S2"], small=True)
        c.emit("vector", lambda e: e.engine_nop(),
               reads=[pfx + "Q2c%d" % t_ for t_ in range(NT)],
               writes=[pfx + "Q2"], small=True)
        if Core.DEBUG and l == 0:
            c.dump(pfx + "z1", z[:], [pfx + "z%d" % t_ for t_ in range(NT)])
        stat_sh_shift()
        s1deps = [pfx + "S1%d" % t_ for t_ in range(NT)] + [pfx + "shA", pfx + "shB"]
        q1deps = [pfx + "Q1%d" % t_ for t_ in range(NT)] + [pfx + "shC", pfx + "shD"]
        mu_s = stat[:, 4 * NT:5 * NT]
        rstd_s = stat[:, 5 * NT:6 * NT]
        tA = stat[:, 6 * NT:7 * NT]
        tB = stat[:, 7 * NT:8 * NT]
        V(lambda e: e.tensor_add(tA, scol[:, 0:NT], shcol[:, 0:NT]),
          s1deps, [p+"tA"])
        V(lambda e: e.tensor_scalar_mul(mu_s, tA, 1.0 / D), [p+"tA"], [p+"mus"])
        V(lambda e: e.tensor_add(tB, scol[:, 2 * NT:3 * NT], shcol[:, NT:2 * NT]),
          q1deps, [p+"tB"])
        V(lambda e: e.tensor_scalar_mul(tB, tB, 1.0 / D), [p+"tB"], [p+"tB2"])
        V(lambda e: e.tensor_mul(tA, mu_s, mu_s), [p+"mus"], [p+"tA2"])
        V(lambda e: e.tensor_sub(tB, tB, tA), [p+"tB2", p+"tA2"], [p+"var"])
        V(lambda e: e.tensor_scalar_add(tB, tB, 1e-5), [p+"var"], [p+"vare"])
        A(lambda e: e.activation(tB, tB, AF.Ln), [p+"vare"], [p+"lns"])
        A(lambda e: e.activation(rstd_s, tB, AF.Exp, scale=-0.5),
          [p+"lns"], [p+"rstds"])
        V(lambda e: e.tensor_mul(tA, mu_s, rstd_s), [p+"mus", p+"rstds"],
          [p+"murs"])
        c.emit("gpsimd", lambda e: e.nop(), reads=AREG, writes=BREG)
        build_zf()
        gc = c.load_vec_cols(pfx + "g%d" % l, wd["ln_g"][l, :], D, tmp=vctmp, tmpres=pfx + "vctmp")
        bc = c.load_vec_cols(pfx + "bb%d" % l, wd["ln_b"][l, :], D, tmp=vctmp, tmpres=pfx + "vctmp")
        gcres = pfx + "g%d_c" % l
        bcres = pfx + "bb%d_c" % l
        c.emit("gpsimd", lambda e, l=l: e.dma_start(
            out=b1r[0:1, 0:5460], in_=wd["b1"][l:l + 1, :]),
            writes=[pfx + "b1r"], dma_slot="vload_g")
        c.emit("gpsimd", lambda e, l=l: e.dma_start(
            out=b2r[0:1, :], in_=wd["b2"][l:l + 1, :]),
            writes=[pfx + "b2r"], dma_slot="vload_g")

        if Core.DEBUG and l == 0:
            c.dump(pfx + "statd", stat[:], [pfx + "rstds", pfx + "murs"])

        NJ = 22
        for tb in range(4):
            for nm3, colap, rowap, rres in (
                    ("rstds", rstd_s, rrow, pfx + "rr"),
                    ("murs", tA, mrow, pfx + "mr")):
                for jj in range(4):
                    c.emit("tensor", lambda e, colap=colap, tb=tb, jj=jj:
                           e.transpose(c.ps[6][0:1, jj * 128:(jj + 1) * 128],
                                       colap[:, tb * 4 + jj: tb * 4 + jj + 1],
                                       c.ident[:]),
                           reads=[pfx + nm3, "consts"], writes=["psB6"])
                    c.emit("vector", lambda e, rowap=rowap, jj=jj: e.tensor_copy(
                        rowap[0:1, jj * 128:(jj + 1) * 128],
                        c.ps[6][0:1, jj * 128:(jj + 1) * 128]),
                        reads=["psB6"], writes=[rres], small=True)
            c.emit("tensor", lambda e: e.matmul(
                c.ps[2][:, :], c.ones_rf[:], rrow[0:1, :],
                start=True, stop=True),
                reads=[pfx + "rr", "consts"], writes=["psB2"])
            c.emit("vector", lambda e: e.tensor_copy(rstdb[:], c.ps[2][:, :]),
                   reads=["psB2"], writes=[pfx + "rstdb"])
            c.emit("tensor", lambda e: e.matmul(
                c.ps[3][:, :], c.ones_rf[:], mrow[0:1, :],
                start=True, stop=True),
                reads=[pfx + "mr", "consts"], writes=["psB3"])
            c.emit("vector", lambda e: e.tensor_copy(mub[:], c.ps[3][:, :]),
                   reads=["psB3"], writes=[pfx + "mub"])
            if Core.DEBUG and l == 0 and tb == 0:
                c.dump(pfx + "rrowd", rrow[:], [pfx + "rr"])
                c.dump(pfx + "rstdbd", rstdb[:], [pfx + "rstdb"])
            for cc in range(ND):
                shift = cc >= 4
                off = (0 if shift else 1) + tb * 512
                c.emit("vector", lambda e, cc=cc, off=off: e.tensor_mul(
                    geb[:], zfc(cc)[:, off:off + 512], rstdb[:]),
                    reads=[pfx + "zf%d" % cc, pfx + "zf0", pfx + "rstdb"],
                    writes=[pfx + "geb"])
                c.emit("vector", lambda e: e.tensor_sub(geb[:], geb[:], mub[:]),
                       reads=[pfx + "geb", pfx + "mub"], writes=[pfx + "geb"])
                c.emit("scalar", lambda e, cc=cc, gc=gc, bc=bc: e.activation(
                    h2n[:, cc * 512:(cc + 1) * 512], geb[:], AF.Identity,
                    scale=gc[:, cc:cc + 1], bias=bc[:, cc:cc + 1]),
                    reads=[pfx + "geb", gcres, bcres],
                    writes=[pfx + "h2n%d" % cc])
            h2deps = [pfx + "h2n%d" % c_ for c_ in range(ND)]
            if Core.DEBUG and l == 0 and tb == 0:
                c.dump(pfx + "h2nd", h2n[:], h2deps)
                c.dump(pfx + "gebd", geb[:], [pfx + "geb"])
                c.dump(pfx + "gcd", gc[:], [gcres])
            for j in range(NJ):
                wa = min(128, HID - 128 * j)
                ac0 = 128 * j
                gc0 = HID + 128 * j
                slot = (j % 2) * ND * 128
                c.emit("gpsimd", lambda e, ac0=ac0, wa=wa, slot=slot, l=l:
                       e.dma_start(
                    out=w1blk[:, slot:slot + ND * 128].rearrange(
                        "p (cq m) -> p cq m", m=128)[:, :, 0:wa],
                    in_=wd["w1"][l][:, ac0:ac0 + wa].rearrange(
                        "(cq pp) m -> pp cq m", pp=128)),
                    writes=[pfx + "w1a%d" % (j % 2)], dma_slot="wload")
                c.emit("gpsimd", lambda e, gc0=gc0, wa=wa, slot=slot, l=l:
                       e.dma_start(
                    out=w1blk[:, ND * 256 + slot: ND * 256 + slot + ND * 128]
                    .rearrange("p (cq m) -> p cq m", m=128)[:, :, 0:wa],
                    in_=wd["w1"][l][:, gc0:gc0 + wa].rearrange(
                        "(cq pp) m -> pp cq m", pp=128)),
                    writes=[pfx + "w1g%d" % (j % 2)], dma_slot="wload2")
                for part in range(2):
                    bb = 4 + part
                    base = part * ND * 256 + slot
                    res_w = pfx + ("w1a%d" if part == 0 else "w1g%d") % (j % 2)
                    for mc in range(ND):
                        c.emit("tensor", lambda e, mc=mc, wa=wa, base=base,
                               bb=bb: e.matmul(
                                   c.ps[bb][0:wa, :],
                                   w1blk[:, base + mc * 128: base + mc * 128 + wa],
                                   h2n[:, mc * 512:(mc + 1) * 512],
                                   start=(mc == 0), stop=False),
                               reads=[res_w] + h2deps, writes=["psB%d" % bb])
                    c.emit("tensor", lambda e, part=part, wa=wa, ac0=ac0,
                           gc0=gc0, bb=bb: e.matmul(
                               c.ps[bb][0:wa, :],
                               b1r[0:1, (ac0 if part == 0 else gc0):
                                   (ac0 if part == 0 else gc0) + wa],
                               c.ones_r[0:1, 0:512], start=False, stop=True),
                           reads=[pfx + "b1r", "consts"], writes=["psB%d" % bb])
                c.emit("scalar", lambda e, wa=wa: e.activation(
                    geb[0:wa, :], c.ps[5][0:wa, :], AF.Gelu),
                    reads=["psB5"], writes=[pfx + "gel"])
                c.emit("vector", lambda e, j=j, wa=wa: e.tensor_mul(
                    h3[0:wa, j * 512:(j + 1) * 512],
                    c.ps[4][0:wa, :], geb[0:wa, :]),
                    reads=["psB4", pfx + "gel"], writes=[pfx + "h3_%d" % j])
            if Core.DEBUG and l == 0 and tb == 0:
                c.dump(pfx + "h3d", h3[:], [pfx + "h3_%d" % j_ for j_ in range(NJ)])

            for oc in range(2):
                c.emit("gpsimd", lambda e, oc=oc, l=l: e.dma_start(
                    out=w2hblk[:, 0:11 * 512].rearrange(
                        "p (cq m) -> p cq m", m=512),
                    in_=wd["w2"][l][0:1408, oc * 512:(oc + 1) * 512].rearrange(
                        "(cq pp) m -> pp cq m", pp=128)),
                    writes=[pfx + "w2hA"], dma_slot="wload")
                c.emit("gpsimd", lambda e, oc=oc, l=l: e.dma_start(
                    out=w2hblk[:, 11 * 512:21 * 512].rearrange(
                        "p (cq m) -> p cq m", m=512),
                    in_=wd["w2"][l][1408:2688, oc * 512:(oc + 1) * 512].rearrange(
                        "(cq pp) m -> pp cq m", pp=128)),
                    writes=[pfx + "w2hB"], dma_slot="wload2")
                c.emit("gpsimd", lambda e, oc=oc, l=l: e.dma_start(
                    out=w2hblk[0:42, 21 * 512:22 * 512],
                    in_=wd["w2"][l][2688:2730, oc * 512:(oc + 1) * 512]),
                    writes=[pfx + "w2hB"], dma_slot="wload2")
                for twl in range(4):
                    tt = tb * 4 + twl
                    bb = twl % 2
                    for mc in range(NJ):
                        wa = min(128, HID - 128 * mc)
                        c.emit("tensor", lambda e, twl=twl, mc=mc, wa=wa,
                               bb=bb: e.matmul(
                                   c.ps[bb][:, :],
                                   h3[0:wa, mc * 512 + twl * 128:
                                      mc * 512 + twl * 128 + 128],
                                   w2hblk[0:wa, mc * 512:(mc + 1) * 512],
                                   start=(mc == 0), stop=False),
                               reads=[pfx + "h3_%d" % mc,
                                      pfx + ("w2hA" if mc < 11 else "w2hB")],
                               writes=["psB%d" % bb])
                    c.emit("tensor", lambda e, oc=oc, bb=bb: e.matmul(
                        c.ps[bb][:, :], c.ones_r[0:1, 0:128],
                        b2r[0:1, oc * 512:(oc + 1) * 512],
                        start=False, stop=True),
                        reads=[pfx + "b2r", "consts"], writes=["psB%d" % bb])
                    c.emit("vector", lambda e, tt=tt, oc=oc, bb=bb: e.tensor_add(
                        z[:, tt * D + oc * 512: tt * D + oc * 512 + 512],
                        z[:, tt * D + oc * 512: tt * D + oc * 512 + 512],
                        c.ps[bb][:, :]),
                        reads=["psB%d" % bb, pfx + "z%d" % tt],
                        writes=[pfx + "z%d" % tt])

    c.dump(pfx + "zf_fin", z[:], [pfx + "z%d" % t_ for t_ in range(NT)])

    for tt in range(NT):
        sl = xb[:, 0:D]
        xres = pfx + "xs"
        c.emit("sync", lambda e, s=sl, tt=tt: e.dma_start(
            out=s, in_=xin[tt * 128:(tt + 1) * 128, :]),
            writes=[xres], dma_slot="x0")
        c.emit("vector", lambda e, s=sl, tt=tt: e.scalar_tensor_tensor(
            s, z[:, tt * D:(tt + 1) * D], 0.5, s, op0=ALU.mult, op1=ALU.add),
            reads=[pfx + "z%d" % tt, xres], writes=[xres])
        c.emit("sync", lambda e, s=sl, tt=tt: e.dma_start(
            out=xout[tt * 128:(tt + 1) * 128, :], in_=s),
            reads=[xres], dma_slot="st0")


def _build_l0ffn_program():
    es = contextlib.ExitStack()
    with es:
        nc = bass.Bass("TRN2", target_bir_lowering=False, detect_race_conditions=False)
        xin = nc.dram_tensor("x", [T, D], F32, kind="ExternalInput").ap()
        xout = nc.dram_tensor("xo", [T, D], F32, kind="ExternalOutput").ap()
        wd = {}
        for k, shp in [("wqkv", [2, D, 1536]), ("q_scale", [2, 64]),
                       ("k_scale", [2, 64]), ("wo", [2, 512, D]),
                       ("ln_g", [2, D]), ("ln_b", [2, D]),
                       ("w1", [2, D, 5460]), ("b1", [2, 5460]),
                       ("w2", [2, 2730, D]), ("b2", [2, D])]:
            wd[k] = nc.dram_tensor(k, shp, F32, kind="ExternalInput").ap()
        c = Core(nc, es)
        emit_l0ffn(c, xin, xout, wd)
        c.end_barrier()
        stats = c.finalize()
    return nc, stats


def run_l0ffn(x_full, w):
    nc, stats = _build_l0ffn_program()
    in_maps = []
    for b in range(8):
        m = {"x": np.ascontiguousarray(x_full[b])}
        m.update(w)
        in_maps.append(m)
    r = run_bass_kernel_spmd(nc, in_maps, core_ids=list(range(8)))
    return np.stack([r.results[b]["xo"] for b in range(8)])
